# version 13
# baseline (speedup 1.0000x reference)
import numpy as np
from contextlib import ExitStack

import concourse.bass as bass
import concourse.mybir as mybir
from concourse.bass_utils import run_bass_kernel_spmd

F32 = mybir.dt.float32
BF16 = mybir.dt.bfloat16
I32 = mybir.dt.int32
U8 = mybir.dt.uint8
AF = mybir.ActivationFunctionType
ALU = mybir.AluOpType

HID = 2048
INTER = 4096
CH = 256
NCH = 4
T = 1024
TE = T + 3
PROJ = 10304
FH = 5632
EPS_LN = 1e-5
EPS_RMS = 1e-6

ENGS = ("sync", "gpsimd", "tensor", "vector", "scalar")


class B:
    def __init__(self):
        self.prog = {e: [] for e in ENGS}
        self.cnt = {}

    def op(self, eng, fn, inc=None, amt=1):
        if inc is not None and (inc.startswith("sW") or inc.startswith("sS")
                                or inc.startswith("sD") or inc.startswith("sE")
                                or inc in ("sX", "sG", "sZ")):
            amt = 16
        if inc is not None:
            self.cnt[inc] = self.cnt.get(inc, 0) + amt

            def g(e, sems, fn=fn, inc=inc, amt=amt):
                fn(e).then_inc(sems[inc], amt)
        else:
            def g(e, sems, fn=fn):
                fn(e)
        self.prog[eng].append(g)

    def ccop(self, eng, fn, inc):
        self.cnt[inc] = self.cnt.get(inc, 0) + 1

        def g(e, sems, fn=fn, inc=inc):
            fn(e).then_inc(sems[inc])
        self.prog[eng].append(g)

    def wait(self, eng, sem, val):
        if val <= 0:
            return

        def g(e, sems, sem=sem, val=val):
            e.wait_ge(sems[sem], val)
        self.prog[eng].append(g)

    def c(self, sem):
        return self.cnt.get(sem, 0)

    def barrier(self, engs=ENGS):
        snap = dict(self.cnt)
        for eng in engs:
            for s, v in snap.items():
                if v > 0 and s != "sCC":
                    self.wait(eng, s, v)


def build_nc():
    nc = bass.Bass("TRN2", target_bir_lowering=False, num_devices=8)

    xT = nc.dram_tensor("xT", [HID, TE], F32, kind="ExternalInput").ap()
    wip = nc.dram_tensor("wip", [HID, PROJ], F32, kind="ExternalInput").ap()
    wop = nc.dram_tensor("wop", [INTER, HID], F32, kind="ExternalInput").ap()
    wcv = nc.dram_tensor("wcv", [128, 48, 4], F32, kind="ExternalInput").ap()
    wcb = nc.dram_tensor("wcb", [128, 48], F32, kind="ExternalInput").ap()
    dtb = nc.dram_tensor("dtb", [64, 1], F32, kind="ExternalInput").ap()
    alog = nc.dram_tensor("alog", [64, 1], F32, kind="ExternalInput").ap()
    dvec = nc.dram_tensor("dvec", [1, 64], F32, kind="ExternalInput").ap()
    dvec2 = nc.dram_tensor("dvec2", [128, 32], F32, kind="ExternalInput").ap()
    nw = nc.dram_tensor("nw", [128, 32], F32, kind="ExternalInput").ap()
    wg = nc.dram_tensor("wg", [HID, FH], I32, kind="ExternalInput").ap()
    wu = nc.dram_tensor("wu", [HID, FH], I32, kind="ExternalInput").ap()
    wd = nc.dram_tensor("wd", [FH, HID], I32, kind="ExternalInput").ap()
    sg = nc.dram_tensor("sg", [128, 44], F32, kind="ExternalInput").ap()
    su = nc.dram_tensor("su", [128, 44], F32, kind="ExternalInput").ap()
    sdn = nc.dram_tensor("sdn", [128, 16], F32, kind="ExternalInput").ap()
    wsel = nc.dram_tensor("wsel", [128, 8], F32, kind="ExternalInput").ap()

    outT = nc.dram_tensor("outT", [HID, T], F32, kind="ExternalOutput").ap()

    acumD = nc.dram_tensor("acumD", [64, T], F32, kind="Internal").ap()
    acumeD = nc.dram_tensor("acumeD", [64, T], F32, kind="Internal").ap()
    dtD = nc.dram_tensor("dtD", [64, T], F32, kind="Internal").ap()
    dtdD = nc.dram_tensor("dtdD", [64, T], F32, kind="Internal").ap()
    aeD = nc.dram_tensor("aeD", [4, 64], F32, kind="Internal").ap()
    zDr = nc.dram_tensor("zDr", [NCH, 128, INTER], BF16, kind="Internal").ap()
    gateD = nc.dram_tensor("gateD", [32, 128, T], BF16, kind="Internal").ap()
    x1D = nc.dram_tensor("x1D", [16, 128, T], F32, kind="Internal").ap()
    ccI = nc.dram_tensor("ccI", [128, INTER], BF16, kind="Internal").ap()
    ccO = nc.dram_tensor("ccO", [8 * 128, INTER], BF16, kind="Internal").ap()

    es = ExitStack()
    with es:
        A = lambda *a: es.enter_context(nc.sbuf_tensor(*a))
        hbc = A("hbc", [128, 48, T], BF16)
        xb = A("xb", [128, 16, TE], BF16)
        wb1 = A("wb1", [128, 8192], BF16)
        uniB = A("uniB", [128, 29184], U8)
        uniC = A("uniC", [128, 16384], U8)
        uniR1 = A("uniR1", [128, 6144], U8)
        uniR2 = A("uniR2", [128, 4108], U8)
        ring = A("ring", [128, 2, 1028], BF16)
        idb = A("idb", [128, 128], BF16)
        idf = A("idf", [128, 128], F32)
        maskA = A("maskA", [128, 256], BF16)
        ones64 = A("ones64", [64, 256], BF16)
        ones1f = A("ones1f", [1, 128], F32)
        onescb = A("onescb", [128, 1], BF16)
        convw = A("convw", [128, 48, 4], F32)
        convb = A("convb", [128, 48], F32)
        dtbS = A("dtbS", [64, 1], F32)
        alogS = A("alogS", [64, 1], F32)
        eaS = A("eaS", [64, 1], F32)
        drep = A("drep", [128, 64], F32)
        drep2 = A("drep2", [128, 32], F32)
        nwS = A("nwS", [128, 32], F32)
        sgS = A("sgS", [128, 44], F32)
        suS = A("suS", [128, 44], F32)
        sdS = A("sdS", [128, 16], F32)
        wselS = A("wselS", [128, 8], F32)
        cstS = A("cstS", [1, 2], F32)

        def v(arena, off, nbytes, dt, shape, parts=128):
            ap = arena[0:parts, off:off + nbytes].bitcast(dt)
            if len(shape) > 1:
                kw = {}
                names = "abc"[:len(shape) - 1]
                pat = "p (" + " ".join(names) + ") -> p " + " ".join(names)
                for nm, s in zip(names, shape[1:]):
                    kw[nm] = s
                ap = ap.rearrange(pat, **kw)
            return ap

        convS = v(uniB, 0, 2 * TE * 4, F32, [128, 2, TE])
        rep8h = [v(uniB, 0, 4096, F32, [128, 4, 256]),
                 v(uniB, 4096, 4096, F32, [128, 4, 256])]
        wb2 = v(uniB, 0, 11264, BF16, [128, 2, 44, 64])
        dtf = v(uniB, 8320, 4096, F32, [64, 1024], parts=64)
        acum = v(uniB, 12544, 4096, F32, [64, 1024], parts=64)
        scr64 = v(uniB, 16704, 4096, F32, [64, 1024], parts=64)

        corr = v(uniB, 12544, 8192, F32, [1, 2, 1024], parts=1)
        cacc = v(uniB, 20864, 8192, F32, [128, 2, 1024])
        dsb = v(uniB, 20864, 6144, F32, [128, 4, 384])
        zbuf = v(uniB, 20864, 8192, BF16, [128, 4096])
        msb = v(uniB, 8320, 3072, BF16, [128, 4, 384])
        dtr2 = v(uniB, 8320, 1024, BF16, [128, 2, 256])
        maskB = maskA[:, 0:128]

        s_cur = v(uniC, 0, 8192, BF16, [128, 4096])
        s_nxt = v(uniC, 8192, 8192, BF16, [128, 4096])
        t64 = v(uniC, 0, 8192, F32, [128, 2, 1024])
        t64x4 = v(uniC, 0, 16384, F32, [128, 4, 1024])
        repf = v(uniC, 8192, 8192, F32, [128, 2, 1024])

        rows3 = v(uniR1, 0, 6144, F32, [1, 3, 512], parts=1)
        ttile = v(uniR1, 0, 1024, BF16, [128, 4, 128])
        zrow = v(uniR1, 1024, 2048, BF16, [128, 2, 512])
        acumT = v(uniR1, 3072, 2048, F32, [128, 8, 64])
        aes = v(uniR1, 5120, 1024, F32, [128, 256])
        yscr = v(uniR1, 1024, 1024, F32, [64, 256], parts=64)
        yscrH = uniR1[64:128, 1024:2048].bitcast(F32)
        ysF = uniR1[0:128, 1024:2048].bitcast(F32)

        reps = v(uniR2, 0, 2 * TE * 2, BF16, [128, 2, TE])
        btile = v(uniR2, 0, 4096, BF16, [128, 8, 2, 128])
        sgr = v(uniB, 0, 4096, BF16, [128, 2, 2, 512])
        esb = v(uniR2, 0, 3072, BF16, [128, 4, 384])

        psA = es.enter_context(nc.psum_tensor("psA", [128, 7, 512], F32))
        psT = es.enter_context(nc.psum_tensor("psT", [128, 8, 128], BF16))

        sems = {}
        for s in ("sWs", "sWg", "sT", "sV", "sA", "sP", "sCC",
                  "sX", "sG", "sZ", "sW0", "sW1", "sW2", "sW3", "sS0", "sS1",
                  "sD0", "sD1", "sE0", "sE1"):
            sems[s] = es.enter_context(nc.semaphore(s))
        block = es.enter_context(nc.Block())
        b = B()

        for buf in (idb, idf):
            b.op("gpsimd", lambda e, t=buf: e.memset(t[:, :], 0.0), "sP")
            b.op("gpsimd", lambda e, t=buf: e.affine_select(
                out=t[:, :], in_=t[:, :], compare_op=ALU.not_equal, fill=1.0,
                base=0, pattern=[[-1, 128]], channel_multiplier=1), "sP")
        for buf, n in ((maskA, 256),):
            b.op("gpsimd", lambda e, t=buf: e.memset(t[:, :], 0.0), "sP")
            b.op("gpsimd", lambda e, t=buf, n=n: e.affine_select(
                out=t[:, :], in_=t[:, :], compare_op=ALU.is_ge, fill=-1e30,
                base=0, pattern=[[1, n]], channel_multiplier=-1), "sP")
        b.op("gpsimd", lambda e: e.memset(cstS[0:1, 0:1], EPS_LN), "sP")
        b.op("gpsimd", lambda e: e.memset(cstS[0:1, 1:2], EPS_RMS), "sP")
        nc.const_aps.register(F32, EPS_LN, cstS[0:1, 0:1])
        nc.const_aps.register(F32, EPS_RMS, cstS[0:1, 1:2])
        for buf in (ones64, ones1f, onescb):
            b.op("gpsimd", lambda e, t=buf: e.memset(
                t[tuple(slice(None) for _ in t.shape)], 1.0), "sP")
        consts_done = b.c("sP")

        for dst, src in ((convw, wcv), (convb, wcb), (dtbS, dtb), (alogS, alog),
                         (nwS, nw), (sgS, sg), (suS, su), (sdS, sdn), (wselS, wsel)):
            b.op("sync", lambda e, d=dst, s=src: e.dma_start(
                out=d[tuple(slice(None) for _ in d.shape)], in_=s), "sWs")
        b.op("sync", lambda e: e.dma_start(
            out=drep[:, :], in_=dvec.broadcast_to([128, 64])), "sWs")
        b.op("sync", lambda e: e.dma_start(out=drep2[:, :], in_=dvec2), "sWs")
        ld_small = b.c("sWs")
        b.wait("scalar", "sWs", ld_small)
        b.wait("scalar", "sP", b.c("sP"))
        b.op("scalar", lambda e: e.activation(eaS[:, :], alogS[:, :], AF.Exp), "sA")

        for d in range(16):
            b.op("gpsimd", lambda e, d=d: e.dma_start(
                out=xb[:, d, :], in_=xT[128 * d:128 * (d + 1), :]), "sX")
        ld_x = b.c("sX")

        IP_BLOCKS = [(4096 + 256 * i, 256) for i in range(24)]
        IP_BLOCKS.append((10240, 64))
        IP_BLOCKS += [(256 * i, 256) for i in range(16)]
        wipR = wip.rearrange("(dc p) h -> p dc h", p=128)
        wb1v = wb1.rearrange("p (a b c) -> p a b c", a=2, b=16)
        ipw_done = []
        ipw_cons = []

        def load_ip_block(i):
            c0, w = IP_BLOCKS[i]
            sem = "sW%d" % (i % 2)
            if i >= 2:
                b.wait("gpsimd", "sT", ipw_cons[i - 2])
            b.op("gpsimd", lambda e, c0=c0, w=w, i=i: e.dma_start(
                out=wb1v[:, i % 2, :, 0:w], in_=wipR[:, :, c0:c0 + w]), sem)
            ipw_done.append((sem, b.c(sem)))

        load_ip_block(0)
        load_ip_block(1)

        def stats_rep(grp, nfeat, eps, with_mean, bf_dst=None, f_dst=None):
            ng = len(grp)
            for gi, (o, w) in enumerate(grp):
                if with_mean:
                    b.op("vector", lambda e, gi=gi, w=w: e.tensor_scalar(
                        psA[0:1, gi, 0:w], psA[0:1, gi, 0:w], 1.0 / nfeat, None,
                        ALU.mult), "sV")
                b.op("vector", lambda e, gi=gi, w=w, ng=ng: e.tensor_scalar(
                    psA[0:1, ng + gi, 0:w], psA[0:1, ng + gi, 0:w], 1.0 / nfeat, None,
                    ALU.mult), "sV")
                if with_mean:
                    b.wait("scalar", "sV", b.c("sV"))
                    b.op("scalar", lambda e, gi=gi, w=w: e.activation(
                        rows3[0:1, 2, 0:w], psA[0:1, gi, 0:w], AF.Square), "sA")
                    b.wait("vector", "sA", b.c("sA"))
                    b.op("vector", lambda e, gi=gi, w=w, ng=ng: e.tensor_sub(
                        psA[0:1, ng + gi, 0:w], psA[0:1, ng + gi, 0:w],
                        rows3[0:1, 2, 0:w]), "sV")
                b.wait("scalar", "sV", b.c("sV"))
                b.op("scalar", lambda e, gi=gi, w=w, ng=ng, eps=eps: e.activation(
                    rows3[0:1, 2, 0:w], psA[0:1, ng + gi, 0:w], AF.Ln, bias=eps), "sA")
                b.op("scalar", lambda e, w=w: e.activation(
                    rows3[0:1, 0, 0:w], rows3[0:1, 2, 0:w], AF.Exp, scale=-0.5), "sA")
                if with_mean:
                    b.wait("vector", "sA", b.c("sA"))
                    b.op("vector", lambda e, gi=gi, w=w: e.tensor_mul(
                        rows3[0:1, 1, 0:w], psA[0:1, gi, 0:w], rows3[0:1, 0, 0:w]), "sV")
                nrep = 2 if with_mean else 1
                for ri in range(nrep):
                    b.wait("tensor", "sA" if ri == 0 else "sV",
                           b.c("sA") if ri == 0 else b.c("sV"))
                    b.op("tensor", lambda e, w=w, ri=ri: e.matmul(
                        psA[:, 6, 0:w], ones1f[:, :], rows3[0:1, ri, 0:w],
                        start=True, stop=True), "sT")
                    b.wait("vector", "sT", b.c("sT"))
                    if bf_dst is not None:
                        b.op("vector", lambda e, o=o, w=w, ri=ri, dd=bf_dst: e.tensor_copy(
                            dd[:, ri, o:o + w], psA[:, 6, 0:w]), "sV")
                    if f_dst is not None:
                        b.op("vector", lambda e, o=o, w=w, ri=ri, dd=f_dst: e.tensor_copy(
                            dd[:, ri, o:o + w], psA[:, 6, 0:w]), "sV")
                    b.wait("tensor", "sV", b.c("sV"))

        GRP3 = ((0, 512), (512, 512), (1024, 3))
        GRP2 = ((0, 512), (512, 512))
        b.wait("tensor", "sP", consts_done)
        b.wait("tensor", "sX", ld_x)
        b.wait("vector", "sX", ld_x)
        sqm = {}
        for d in range(16):
            if d >= 2:
                b.wait("vector", "sT", sqm[d - 2])
            b.op("vector", lambda e, d=d: e.tensor_mul(
                ring[:, d % 2, 0:TE], xb[:, d, :], xb[:, d, :]), "sV")
            b.wait("tensor", "sV", b.c("sV"))
            for gi, (o, w) in enumerate(GRP3):
                b.op("tensor", lambda e, d=d, gi=gi, o=o, w=w: e.matmul(
                    psA[0:1, gi, 0:w], onescb[:, :], xb[:, d, o:o + w],
                    start=(d == 0), stop=(d == 15)), None)
            for gi, (o, w) in enumerate(GRP3):
                b.op("tensor", lambda e, d=d, gi=gi, o=o, w=w: e.matmul(
                    psA[0:1, 3 + gi, 0:w], onescb[:, :], ring[:, d % 2, o:o + w],
                    start=(d == 0), stop=(d == 15)), "sT" if gi == 2 else None)
            sqm[d] = b.c("sT")
        b.wait("vector", "sT", b.c("sT"))
        stats_rep(GRP3, HID, EPS_LN, True, bf_dst=reps)
        for d in range(16):
            b.op("vector", lambda e, d=d: e.tensor_mul(
                ring[:, d % 2, 0:TE], xb[:, d, :], reps[:, 0, :]), "sV")
            b.op("vector", lambda e, d=d: e.tensor_sub(
                xb[:, d, :], ring[:, d % 2, 0:TE], reps[:, 1, :]), "sV")
        xn1_done = b.c("sV")

        b.wait("tensor", "sV", xn1_done)
        silu_cnt = {}
        bank_free = {j: 0 for j in range(6)}
        gcount = 0
        for i in range(24):
            b.wait("tensor", ipw_done[i][0], ipw_done[i][1])
            for ht in range(2):
                k = 2 * i + ht
                for gi, (o, w) in enumerate(GRP3):
                    bank = gcount % 6
                    gcount += 1
                    b.wait("tensor", "sV", bank_free[bank])
                    for d in range(16):
                        b.op("tensor", lambda e, d=d, o=o, w=w, i=i, ht=ht, bank=bank: e.matmul(
                            psA[:, bank, 0:w], wb1v[:, i % 2, d, 128 * ht:128 * (ht + 1)],
                            xb[:, d, o:o + w], start=(d == 0), stop=(d == 15)),
                            "sT" if d == 15 else None)
                    b.wait("vector", "sT", b.c("sT"))
                    b.op("vector", lambda e, o=o, w=w, k=k, bank=bank: e.tensor_copy(
                        convS[:, k % 2, o:o + w], psA[:, bank, 0:w]), "sV")
                    bank_free[bank] = b.c("sV")
                if k >= 1:
                    b.wait("vector", "sA", silu_cnt[k - 1])
                b.op("vector", lambda e, k=k: e.tensor_scalar(
                    cacc[:, 0, :], convS[:, k % 2, 0:T], convw[:, k, 0:1], None, ALU.mult), "sV")
                b.op("vector", lambda e, k=k: e.scalar_tensor_tensor(
                    cacc[:, 1, :], convS[:, k % 2, 1:1 + T], convw[:, k, 1:2],
                    cacc[:, 0, :], ALU.mult, ALU.add), "sV")
                b.op("vector", lambda e, k=k: e.scalar_tensor_tensor(
                    cacc[:, 0, :], convS[:, k % 2, 2:2 + T], convw[:, k, 2:3],
                    cacc[:, 1, :], ALU.mult, ALU.add), "sV")
                b.op("vector", lambda e, k=k: e.scalar_tensor_tensor(
                    cacc[:, 1, :], convS[:, k % 2, 3:3 + T], convw[:, k, 3:4],
                    cacc[:, 0, :], ALU.mult, ALU.add), "sV")
                b.wait("scalar", "sV", b.c("sV"))
                b.op("scalar", lambda e, k=k: e.activation(
                    hbc[:, k, :], cacc[:, 1, :], AF.Silu, bias=convb[:, k:k + 1]), "sA")
                silu_cnt[k] = b.c("sA")
            ipw_cons.append(b.c("sT"))
            if i + 2 < len(IP_BLOCKS):
                load_ip_block(i + 2)

        b.wait("tensor", ipw_done[24][0], ipw_done[24][1])
        b.wait("tensor", "sV", bank_free[0])
        b.wait("scalar", "sWs", ld_small)
        for t in range(2):
            for d in range(16):
                b.op("tensor", lambda e, d=d, t=t: e.matmul(
                    psA[0:64, 0, 0:512], wb1v[:, 0, d, 0:64],
                    xb[:, d, 1 + 512 * t:1 + 512 * (t + 1)],
                    start=(d == 0), stop=(d == 15)), "sT" if d == 15 else None)
            b.wait("scalar", "sT", b.c("sT"))
            b.op("scalar", lambda e, t=t: e.activation(
                scr64[:, 512 * t:512 * (t + 1)], psA[0:64, 0, 0:512], AF.Exp,
                bias=dtbS[:, 0:1]), "sA")
            b.wait("tensor", "sA", b.c("sA"))
        ipw_cons.append(b.c("sT"))
        load_ip_block(26)
        b.op("scalar", lambda e: e.activation(dtf[:, :], scr64[:, :], AF.Ln, bias=1.0), "sA")
        b.wait("vector", "sA", b.c("sA"))
        b.op("vector", lambda e: e.tensor_scalar_max(dtf[:, :], dtf[:, :], 0.0), "sV")
        b.op("vector", lambda e: e.tensor_scalar(
            scr64[:, :], dtf[:, :], eaS[:, 0:1], -1.0, ALU.mult, ALU.mult), "sV")
        for c in range(NCH):
            b.op("vector", lambda e, c=c: e.tensor_tensor_scan(
                acum[:, CH * c:CH * (c + 1)], ones64[:, :], scr64[:, CH * c:CH * (c + 1)],
                0.0, ALU.mult, ALU.add), "sV")
        for c in range(NCH):
            b.op("vector", lambda e, c=c: e.tensor_scalar(
                scr64[:, CH * c:CH * (c + 1)], acum[:, CH * c:CH * (c + 1)],
                acum[:, CH * (c + 1) - 1:CH * (c + 1)], -1.0, ALU.subtract, ALU.mult), "sV")
        b.wait("scalar", "sV", b.c("sV"))
        b.op("scalar", lambda e: e.activation(scr64[:, :], scr64[:, :], AF.Exp), "sA")
        b.wait("vector", "sA", b.c("sA"))
        b.op("vector", lambda e: e.tensor_mul(scr64[:, :], scr64[:, :], dtf[:, :]), "sV")
        dtdec_done = b.c("sV")

        b.wait("sync", "sV", dtdec_done)
        b.op("sync", lambda e: e.dma_start(
            out=dtD, in_=dtf[:, :]), "sWs")
        b.op("sync", lambda e: e.dma_start(
            out=dtdD, in_=scr64[:, :]), "sWs")
        b.op("sync", lambda e: e.dma_start(out=acumD, in_=acum[:, :]), "sWs")
        b.op("sync", lambda e: e.dma_start(
            out=aeD.transpose([1, 0]), in_=acum[:, CH - 1::CH],
            allow_slow_non_contiguous=True), "sWs")
        b.op("scalar", lambda e: e.activation(cacc[0:64, 0, :], acum[:, :], AF.Exp), "sA")
        b.wait("sync", "sA", b.c("sA"))
        b.op("sync", lambda e: e.dma_start(out=acumeD, in_=cacc[0:64, 0, :]), "sWs")
        dt_out = b.c("sWs")

        b.wait("tensor", "sV", dtdec_done)
        for cs in range(8):
            b.op("tensor", lambda e, cs=cs: e.transpose(
                psA[:, 4, 0:64], acum[:, 128 * cs:128 * (cs + 1)], idf[0:64, 0:64]), "sT")
            b.wait("vector", "sT", b.c("sT"))
            b.op("vector", lambda e, cs=cs: e.tensor_copy(
                acumT[:, cs, :], psA[:, 4, 0:64]), "sV")
            b.wait("tensor", "sV", b.c("sV"))
        b.wait("sync", "sWs", dt_out)
        b.op("sync", lambda e: e.dma_start(
            out=aes[:, :],
            in_=aeD.rearrange("c h -> (c h)").unsqueeze(0).broadcast_to([128, 256])), "sWs")
        b.wait("scalar", "sWs", b.c("sWs"))
        b.op("scalar", lambda e: e.activation(aes[:, :], aes[:, :], AF.Exp), "sA")

        b.barrier()
        zrow_free = {0: 0, 1: 0}
        dtr2_free = {0: 0, 1: 0}
        dtr2_done = {}

        def load_dtr2(c2, k2):
            slot = k2 % 2
            sem = "sD%d" % slot
            b.wait("gpsimd", "sV", dtr2_free[slot])
            b.op("gpsimd", lambda e, c2=c2, k2=k2, slot=slot: e.dma_start(
                out=dtr2[0:64, slot, :],
                in_=dtdD[2 * k2, CH * c2:CH * (c2 + 1)].unsqueeze(0)
                .broadcast_to([64, CH])), sem)
            b.op("gpsimd", lambda e, c2=c2, k2=k2, slot=slot: e.dma_start(
                out=dtr2[64:128, slot, :],
                in_=dtdD[2 * k2 + 1, CH * c2:CH * (c2 + 1)].unsqueeze(0)
                .broadcast_to([64, CH])), sem)
            dtr2_done[slot] = (sem, b.c(sem))

        for c in range(NCH):
            for g in range(8):
                for lh in range(2):
                    b.op("tensor", lambda e, c=c, g=g, lh=lh: e.transpose(
                        psT[:, (2 * g + lh) % 4, :],
                        hbc[:, 32 + g, CH * c + 128 * lh:CH * c + 128 * (lh + 1)],
                        idb[:, :]), "sT")
                if g % 2 == 1:
                    b.wait("vector", "sT", b.c("sT"))
                    b.op("vector", lambda e, g=g: e.tensor_copy(
                        btile[:, g - 1:g + 1, :, :].rearrange("p a b c -> p (a b c)"),
                        psT[:, 0:4, :].rearrange("p a b -> p (a b)")), "sV")
                    b.wait("tensor", "sV", b.c("sV"))
            load_dtr2(c, 0)
            load_dtr2(c, 1)
            for g in range(8):
                zbank = 5 + (g % 2)
                for kk in range(4):
                    k = 4 * g + kk
                    b.wait("vector", dtr2_done[k % 2][0], dtr2_done[k % 2][1])
                    b.op("vector", lambda e, c=c, k=k: e.tensor_mul(
                        ring[:, k % 2, 0:CH], hbc[:, k, CH * c:CH * (c + 1)],
                        dtr2[:, k % 2, :]), "sV")
                    dtr2_free[k % 2] = b.c("sV")
                    if k + 2 < 32:
                        load_dtr2(c, k + 2)
                    b.wait("tensor", "sV", b.c("sV"))
                    for lh in range(2):
                        b.op("tensor", lambda e, k=k, lh=lh: e.transpose(
                            psT[:, 4 + 2 * (k % 2) + lh, :],
                            ring[:, k % 2, 128 * lh:128 * (lh + 1)], idb[:, :]), "sT")
                    b.wait("vector", "sT", b.c("sT"))
                    b.op("vector", lambda e, k=k: e.tensor_copy(
                        ttile[:, 2 * (k % 2):2 * (k % 2) + 2, :].rearrange("p a b -> p (a b)"),
                        psT[:, 4 + 2 * (k % 2):4 + 2 * (k % 2) + 2, :]
                        .rearrange("p a b -> p (a b)")), "sV")
                    b.wait("tensor", "sV", b.c("sV"))
                    for hh in range(2):
                        h = 2 * k + hh
                        for lh in range(2):
                            b.op("tensor", lambda e, k=k, hh=hh, lh=lh, h=h, zbank=zbank: e.matmul(
                                psA[:, zbank, (h % 8) * 64:(h % 8) * 64 + 64],
                                btile[:, h // 8, lh, :],
                                ttile[:, 2 * (k % 2) + lh, 64 * hh:64 * hh + 64],
                                start=(lh == 0), stop=(lh == 1)),
                                "sT" if lh == 1 else None)
                zg = b.c("sT")
                b.wait("vector", "sT", zg)
                b.wait("vector", "sP", zrow_free[g % 2])
                src = s_cur if c % 2 == 1 else s_nxt
                dst = s_nxt if c % 2 == 1 else s_cur
                b.op("vector", lambda e, g=g, zbank=zbank: e.tensor_copy(
                    zrow[:, g % 2, :], psA[:, zbank, :]), "sV")
                zev = b.c("sV")
                b.wait("gpsimd", "sV", zev)
                if c == 0:
                    b.op("gpsimd", lambda e, g=g, dst=dst: e.tensor_copy(
                        dst[:, 512 * g:512 * (g + 1)], zrow[:, g % 2, :]), "sP")
                else:
                    b.op("gpsimd", lambda e, c=c, g=g, src=src, dst=dst: e.tensor_tensor(
                        dst[:, 512 * g:512 * (g + 1)].rearrange("p (h j) -> p h j", h=8),
                        src[:, 512 * g:512 * (g + 1)].rearrange("p (h j) -> p h j", h=8),
                        aes[:, 64 * c + 8 * g:64 * c + 8 * g + 8].unsqueeze(2)
                        .broadcast_to([128, 8, 64]), ALU.mult), "sP")
                    b.op("gpsimd", lambda e, g=g, dst=dst: e.tensor_tensor(
                        dst[:, 512 * g:512 * (g + 1)], dst[:, 512 * g:512 * (g + 1)],
                        zrow[:, g % 2, :], ALU.add), "sP")
                zrow_free[g % 2] = b.c("sP")
                b.wait("sync", "sV", zev)
                b.op("sync", lambda e, c=c, g=g: e.dma_start(
                    out=zDr[c, :, 512 * g:512 * (g + 1)], in_=zrow[:, g % 2, :]), "sWs")
                b.wait("vector", "sWs", b.c("sWs"))
        zs_done = b.c("sP")

        s_fin = s_nxt
        b.wait("gpsimd", "sV", b.c("sV"))
        b.op("gpsimd", lambda e: e.dma_start(out=ccI, in_=s_fin[:, :]), "sWg")
        b.wait("gpsimd", "sWg", b.c("sWg"))
        b.ccop("gpsimd", lambda e: e.collective_compute(
            "AllGather", ALU.bypass, replica_groups=[list(range(8))],
            ins=[ccI.opt()], outs=[ccO.opt()]), "sCC")
        b.wait("gpsimd", "sCC", b.c("sCC"))
        b.wait("vector", "sP", b.c("sP"))
        for r in range(8):
            b.op("gpsimd", lambda e, r=r: e.dma_start(
                out=zbuf[:, :], in_=ccO[128 * r:128 * (r + 1), :]), "sZ")
            b.wait("vector", "sZ", b.c("sZ"))
            if r == 0:
                b.op("vector", lambda e: e.tensor_scalar(
                    s_cur[:, :], zbuf[:, :], wselS[:, 0:1], None, ALU.mult), "sV")
            else:
                src, dst = (s_cur, s_nxt) if r % 2 == 1 else (s_nxt, s_cur)
                b.op("vector", lambda e, r=r, src=src, dst=dst: e.scalar_tensor_tensor(
                    dst[:, :], zbuf[:, :], wselS[:, r:r + 1], src[:, :],
                    ALU.mult, ALU.add), "sV")
            b.wait("gpsimd", "sV", b.c("sV"))
        s_start0 = s_nxt
        s_other0 = s_cur

        bank_free4 = {j: 0 for j in range(4)}
        sgr_free = {0: ("sS0", 0), 1: ("sS1", 0)}
        for i in range(25, 41):
            b.wait("tensor", ipw_done[i][0], ipw_done[i][1])
            for ht in range(2):
                gt = (IP_BLOCKS[i][0] // 128) + ht
                sl = gt % 2
                for t in range(2):
                    bank = (2 * gt + t) % 4
                    b.wait("tensor", "sA", bank_free4[bank])
                    for d in range(16):
                        b.op("tensor", lambda e, d=d, i=i, ht=ht, t=t, bank=bank: e.matmul(
                            psA[:, bank, :], wb1v[:, i % 2, d, 128 * ht:128 * (ht + 1)],
                            xb[:, d, 1 + 512 * t:1 + 512 * (t + 1)],
                            start=(d == 0), stop=(d == 15)), "sT" if d == 15 else None)
                    b.wait("scalar", "sT", b.c("sT"))
                    if t == 0:
                        b.wait("scalar", sgr_free[sl][0], sgr_free[sl][1])
                    b.op("scalar", lambda e, t=t, sl=sl, bank=bank: e.activation(
                        sgr[:, sl, t, :], psA[:, bank, :], AF.Silu), "sA")
                    bank_free4[bank] = b.c("sA")
                b.wait("sync", "sA", b.c("sA"))
                ssem = "sS%d" % sl
                b.op("sync", lambda e, gt=gt, sl=sl: e.dma_start(
                    out=gateD[gt, :, :],
                    in_=sgr[:, sl, :, :].rearrange("p a b -> p (a b)")), ssem)
                sgr_free[sl] = (ssem, b.c(ssem))
            ipw_cons.append(b.c("sT"))
            if i + 2 < len(IP_BLOCKS):
                load_ip_block(i + 2)

        b.barrier()
        wopR = wop.rearrange("(dc p) h -> p dc h", p=128)
        opw_done = []
        opw_cons = []

        def load_op_block(i):
            sem = "sW%d" % (i % 2)
            if i >= 2:
                b.wait("gpsimd", "sT", opw_cons[i - 2])
            b.op("gpsimd", lambda e, i=i: e.dma_start(
                out=wb1[:, 4096 * (i % 2):4096 * (i % 2) + 4096]
                .rearrange("p (dc h) -> p dc h", dc=32),
                in_=wopR[:, :, 128 * i:128 * (i + 1)]), sem)
            opw_done.append((sem, b.c(sem)))

        load_op_block(0)
        load_op_block(1)

        xbf = xb.rearrange("p a b -> p (a b)")
        dtrt = xbf[:, 1024:1536].rearrange("p (s t) -> p s t", s=2)
        ert = v(uniB, 12544, 2048, F32, [128, 2, 256])
        gatec = xbf[:, 8192:16384].rearrange("p (k t) -> p k t", k=32)

        s_c = s_start0
        s_o = s_other0
        prev_chunk_sV = 0
        ring_free = {0: 0, 1: 0}
        halfq = [(c, g, hf) for c in range(NCH) for g in range(8) for hf in range(2)]
        hload_done = {}
        half_free = {0: 0, 1: 0}

        def load_half(hidx):
            hc, hg, hf = halfq[hidx]
            slot = hidx % 2
            sem = "sD%d" % slot
            hb = 8 * hg + 4 * hf
            b.wait("sync", "sV", half_free[slot])
            b.op("sync", lambda e, hc=hc, hb=hb, slot=slot: e.dma_start(
                out=rep8h[slot][:, :, :],
                in_=acumD[hb:hb + 4, CH * hc:CH * (hc + 1)]
                .unsqueeze(0).broadcast_to([128, 4, CH])), sem)
            hload_done[hidx] = (sem, b.c(sem))

        load_half(0)
        load_half(1)
        dtrt_done = {}

        def load_dtrt(c2, k2):
            slot = k2 % 2
            sem = "sE%d" % slot
            b.op("gpsimd", lambda e, c2=c2, k2=k2, slot=slot: e.dma_start(
                out=dtrt[0:64, slot, :],
                in_=dtD[2 * k2, CH * c2:CH * (c2 + 1)].unsqueeze(0)
                .broadcast_to([64, CH])), sem)
            b.op("gpsimd", lambda e, c2=c2, k2=k2, slot=slot: e.dma_start(
                out=dtrt[64:128, slot, :],
                in_=dtD[2 * k2 + 1, CH * c2:CH * (c2 + 1)].unsqueeze(0)
                .broadcast_to([64, CH])), sem)
            b.op("gpsimd", lambda e, c2=c2, k2=k2, slot=slot: e.dma_start(
                out=ert[0:64, slot, :],
                in_=acumeD[2 * k2, CH * c2:CH * (c2 + 1)].unsqueeze(0)
                .broadcast_to([64, CH])), sem)
            b.op("gpsimd", lambda e, c2=c2, k2=k2, slot=slot: e.dma_start(
                out=ert[64:128, slot, :],
                in_=acumeD[2 * k2 + 1, CH * c2:CH * (c2 + 1)].unsqueeze(0)
                .broadcast_to([64, CH])), sem)
            dtrt_done[slot] = (sem, b.c(sem))

        for c in range(NCH):
            b.wait("sync", "sV", prev_chunk_sV)
            b.op("sync", lambda e, c=c: e.dma_start(
                out=gatec[:, :, :],
                in_=gateD[:, :, CH * c:CH * (c + 1)].rearrange("k p t -> p k t")), "sG")
            gcl = b.c("sG")
            b.wait("gpsimd", "sV", prev_chunk_sV)
            load_dtrt(c, 0)
            load_dtrt(c, 1)
            if c > 0:
                b.op("gpsimd", lambda e, c=c: e.dma_start(
                    out=zbuf[:, :], in_=zDr[c - 1, :, :]), "sZ")
                b.wait("gpsimd", "sZ", b.c("sZ"))
                b.op("gpsimd", lambda e, c=c, src=s_c, dst=s_o: e.tensor_tensor(
                    dst[:, :].rearrange("p (h j) -> p h j", h=64),
                    src[:, :].rearrange("p (h j) -> p h j", h=64),
                    aes[:, 64 * (c - 1):64 * c].unsqueeze(2).broadcast_to([128, 64, 64]),
                    ALU.mult), "sP")
                b.op("gpsimd", lambda e, dst=s_o: e.tensor_tensor(
                    dst[:, :], dst[:, :], zbuf[:, :], ALU.add), "sP")
                s_c, s_o = s_o, s_c
            schain_p = b.c("sP")
            b.wait("tensor", "sP", schain_p)
            b.wait("vector", "sP", schain_p)
            b.wait("vector", "sG", gcl)

            for g in range(8):
                gbank = 4 + (g % 2)
                b.op("tensor", lambda e, c=c, g=g, gbank=gbank: e.matmul(
                    psA[:, gbank, 0:256], hbc[:, 32 + g, CH * c:CH * c + 128],
                    hbc[:, 40 + g, CH * c:CH * (c + 1)], start=True, stop=True), None)
                b.op("tensor", lambda e, c=c, g=g, gbank=gbank: e.matmul(
                    psA[:, gbank, 256:384], hbc[:, 32 + g, CH * c + 128:CH * (c + 1)],
                    hbc[:, 40 + g, CH * c + 128:CH * (c + 1)], start=True, stop=True), "sT")
                gmm = b.c("sT")
                for hf in range(2):
                    hidx = (c * 8 + g) * 2 + hf
                    slot = hidx % 2
                    if hf == 1:
                        b.wait("vector", "sA", b.c("sA"))
                        b.wait("vector", "sT", b.c("sT"))
                    b.wait("vector", hload_done[hidx][0], hload_done[hidx][1])
                    hb = 8 * g + 4 * hf
                    b.op("vector", lambda e, c=c, hb=hb, slot=slot: e.tensor_tensor(
                        dsb[:, :, 0:256], rep8h[slot][:, :, :],
                        acumT[:, 2 * c, hb:hb + 4].unsqueeze(2).broadcast_to([128, 4, CH]),
                        ALU.subtract), "sV")
                    b.op("vector", lambda e: e.tensor_tensor(
                        dsb[:, :, 0:256], dsb[:, :, 0:256],
                        maskA[:, :].unsqueeze(1).broadcast_to([128, 4, CH]), ALU.min), "sV")
                    b.op("vector", lambda e, c=c, hb=hb, slot=slot: e.tensor_tensor(
                        dsb[:, :, 256:384], rep8h[slot][:, :, 128:256],
                        acumT[:, 2 * c + 1, hb:hb + 4].unsqueeze(2).broadcast_to([128, 4, 128]),
                        ALU.subtract), "sV")
                    b.op("vector", lambda e: e.tensor_tensor(
                        dsb[:, :, 256:384], dsb[:, :, 256:384],
                        maskA[:, 0:128].unsqueeze(1).broadcast_to([128, 4, 128]), ALU.min), "sV")
                    dvd = b.c("sV")
                    b.wait("scalar", "sV", dvd)
                    b.op("scalar", lambda e: e.activation(
                        esb.rearrange("p a b -> p (a b)"),
                        dsb.rearrange("p a b -> p (a b)"), AF.Exp), "sA")
                    eexp = b.c("sA")
                    b.wait("vector", "sA", eexp)
                    b.wait("vector", "sT", gmm)
                    b.op("vector", lambda e, gbank=gbank: e.tensor_tensor(
                        msb[:, 0:4, :], esb[:, 0:4, :],
                        psA[:, gbank, 0:384].unsqueeze(1).broadcast_to([128, 4, 384]),
                        ALU.mult), "sV")
                    mdone = b.c("sV")

                    for j in range(2):
                        kk = 2 * hf + j
                        k = 4 * g + kk
                        ybank = kk % 2
                        b.wait("gpsimd", "sA", ring_free[k % 2])
                        b.wait("gpsimd", dtrt_done[k % 2][0], dtrt_done[k % 2][1])
                        b.op("gpsimd", lambda e, c=c, k=k: e.tensor_mul(
                            ring[:, k % 2, 0:CH], hbc[:, k, CH * c:CH * (c + 1)],
                            dtrt[:, k % 2, :]), "sP")
                        b.wait("tensor", "sP", b.c("sP"))
                        for lh in range(2):
                            b.op("tensor", lambda e, k=k, lh=lh: e.transpose(
                                psT[:, 2 * (k % 2) + lh, :],
                                ring[:, k % 2, 128 * lh:128 * (lh + 1)], idb[:, :]), "sT")
                        b.wait("scalar", "sT", b.c("sT"))
                        b.op("scalar", lambda e, k=k: e.copy(
                            ttile[:, 2 * (k % 2):2 * (k % 2) + 2, :].rearrange("p a b -> p (a b)"),
                            psT[:, 2 * (k % 2):2 * (k % 2) + 2, :].rearrange("p a b -> p (a b)")), "sA")
                        tev = b.c("sA")
                        ring_free[k % 2] = tev
                        b.wait("tensor", "sA", tev)
                        b.wait("tensor", "sV", mdone)
                        for hh2 in range(2):
                            h = 2 * k + hh2
                            b.op("tensor", lambda e, c=c, h=h, g=g, ybank=ybank, hh2=hh2, s_c=s_c: e.matmul(
                                psA[64 * hh2:64 * hh2 + 64, ybank, 0:256],
                                s_c[:, 64 * h:64 * (h + 1)],
                                hbc[:, 40 + g, CH * c:CH * (c + 1)],
                                start=True, stop=False), "sT" if hh2 == 1 else None)
                        b.wait("vector", "sT", b.c("sT"))
                        b.op("vector", lambda e, ybank=ybank, k=k: e.tensor_mul(
                            psA[:, ybank, 0:256], psA[:, ybank, 0:256],
                            ert[:, k % 2, :]), "sV")
                        ysc = b.c("sV")
                        b.wait("tensor", "sV", ysc)
                        sl = k % 2
                        for hh2 in range(2):
                            hg = (2 * kk + hh2) % 4
                            b.op("tensor", lambda e, hh2=hh2, hg=hg, sl=sl, ybank=ybank: e.matmul(
                                psA[64 * hh2:64 * hh2 + 64, ybank, 0:256],
                                ttile[:, 2 * sl, 64 * hh2:64 * hh2 + 64],
                                msb[:, hg, 0:256], start=False, stop=False), None)
                            b.op("tensor", lambda e, hh2=hh2, hg=hg, sl=sl, ybank=ybank: e.matmul(
                                psA[64 * hh2:64 * hh2 + 64, ybank, 128:256],
                                ttile[:, 2 * sl + 1, 64 * hh2:64 * hh2 + 64],
                                msb[:, hg, 256:384], start=False, stop=True),
                                "sT" if hh2 == 1 else None)
                        ydg = b.c("sT")
                        b.wait("vector", "sT", ydg)
                        b.op("vector", lambda e, c=c, k=k, ybank=ybank: e.scalar_tensor_tensor(
                            ysF[:, :], hbc[:, k, CH * c:CH * (c + 1)],
                            drep2[:, k:k + 1], psA[:, ybank, 0:256],
                            ALU.mult, ALU.add), "sV")
                        b.wait("vector", "sG", gcl)
                        b.op("vector", lambda e, c=c, k=k: e.tensor_mul(
                            hbc[:, k, CH * c:CH * (c + 1)], ysF[:, :],
                            gatec[:, k, :]), "sV")
                        yev = b.c("sV")
                        b.wait("tensor", "sV", yev)
                        b.wait("gpsimd", "sV", yev)
                        if k + 2 < 32:
                            load_dtrt(c, k + 2)
                    half_free[slot] = b.c("sV")
                    if hidx + 2 < len(halfq):
                        load_half(hidx + 2)
            prev_chunk_sV = b.c("sV")

        b.barrier()
        sqm = {}
        for k in range(32):
            if k >= 2:
                b.wait("vector", "sT", sqm[k - 2])
            b.op("vector", lambda e, k=k: e.tensor_mul(
                ring[:, k % 2, 0:T], hbc[:, k, :], hbc[:, k, :]), "sV")
            b.wait("tensor", "sV", b.c("sV"))
            for t in range(2):
                b.op("tensor", lambda e, k=k, t=t: e.matmul(
                    psA[0:1, 2 + t, 0:512], onescb[:, :], ring[:, k % 2, 512 * t:512 * (t + 1)],
                    start=(k == 0), stop=(k == 31)), "sT" if t == 1 else None)
            sqm[k] = b.c("sT")
        b.wait("vector", "sT", b.c("sT"))
        stats_rep(GRP2, INTER, EPS_RMS, False, bf_dst=reps)
        for k in range(32):
            b.op("vector", lambda e, k=k: e.tensor_scalar(
                hbc[:, k, :], hbc[:, k, :], nwS[:, k:k + 1], None, ALU.mult), "sV")
        ynw = b.c("sV")
        b.wait("tensor", "sV", ynw)

        bank_free6 = {j: 0 for j in range(4)}
        for i in range(16):
            b.wait("tensor", opw_done[i][0], opw_done[i][1])
            for t in range(2):
                bank = (2 * i + t) % 4
                b.wait("tensor", "sV", bank_free6[bank])
                for d in range(32):
                    b.op("tensor", lambda e, i=i, t=t, d=d, bank=bank: e.matmul(
                        psA[:, bank, :],
                        wb1[:, 4096 * (i % 2) + 128 * d:4096 * (i % 2) + 128 * (d + 1)],
                        hbc[:, d, 512 * t:512 * (t + 1)], start=(d == 0), stop=(d == 31)),
                        "sT" if d == 31 else None)
                b.wait("vector", "sT", b.c("sT"))
                b.op("vector", lambda e, i=i, t=t, bank=bank: e.tensor_mul(
                    hbc[:, 32 + i, 512 * t:512 * (t + 1)],
                    psA[:, bank, :], reps[:, 0, 512 * t:512 * (t + 1)]), "sV")
                bank_free6[bank] = b.c("sV")
            opw_cons.append(b.c("sT"))
            if i + 2 < 16:
                load_op_block(i + 2)

        sqm = {}
        for k in range(16):
            if k >= 2:
                b.wait("vector", "sT", sqm[k - 2])
            b.op("vector", lambda e, k=k: e.tensor_mul(
                ring[:, k % 2, 0:T], hbc[:, 32 + k, :], hbc[:, 32 + k, :]), "sV")
            b.wait("tensor", "sV", b.c("sV"))
            for t in range(2):
                b.op("tensor", lambda e, k=k, t=t: e.matmul(
                    psA[0:1, t, 0:512], onescb[:, :], hbc[:, 32 + k, 512 * t:512 * (t + 1)],
                    start=(k == 0), stop=(k == 15)), None)
                b.op("tensor", lambda e, k=k, t=t: e.matmul(
                    psA[0:1, 2 + t, 0:512], onescb[:, :], ring[:, k % 2, 512 * t:512 * (t + 1)],
                    start=(k == 0), stop=(k == 15)), "sT" if t == 1 else None)
            sqm[k] = b.c("sT")
        b.wait("vector", "sT", b.c("sT"))
        stats_rep(GRP2, HID, EPS_LN, True, f_dst=repf)

        for d in range(16):
            b.op("sync", lambda e, d=d: e.dma_start(
                out=t64[:, d % 2, :], in_=xT[128 * d:128 * (d + 1), 1:1 + T]), "sWs")
            xl = b.c("sWs")
            b.op("vector", lambda e, d=d: e.tensor_mul(
                cacc[:, d % 2, :], hbc[:, 32 + d, :], repf[:, 0, :]), "sV")
            b.op("vector", lambda e, d=d: e.tensor_sub(
                cacc[:, d % 2, :], cacc[:, d % 2, :], repf[:, 1, :]), "sV")
            b.wait("vector", "sWs", xl)
            b.op("vector", lambda e, d=d: e.tensor_add(
                t64[:, d % 2, :], t64[:, d % 2, :], cacc[:, d % 2, :]), "sV")
            b.op("vector", lambda e, d=d: e.tensor_copy(xb[:, d, 0:T], t64[:, d % 2, :]), "sV")
            x1v = b.c("sV")
            b.wait("sync", "sV", x1v)
            b.op("sync", lambda e, d=d: e.dma_start(out=x1D[d, :, :], in_=t64[:, d % 2, :]), "sWs")
            b.wait("vector", "sWs", b.c("sWs"))

        b.barrier()
        sqm = {}
        for d in range(16):
            if d >= 2:
                b.wait("vector", "sT", sqm[d - 2])
            b.op("vector", lambda e, d=d: e.tensor_mul(
                ring[:, d % 2, 0:T], xb[:, d, 0:T], xb[:, d, 0:T]), "sV")
            b.wait("tensor", "sV", b.c("sV"))
            for t in range(2):
                b.op("tensor", lambda e, d=d, t=t: e.matmul(
                    psA[0:1, t, 0:512], onescb[:, :], xb[:, d, 512 * t:512 * (t + 1)],
                    start=(d == 0), stop=(d == 15)), None)
                b.op("tensor", lambda e, d=d, t=t: e.matmul(
                    psA[0:1, 2 + t, 0:512], onescb[:, :], ring[:, d % 2, 512 * t:512 * (t + 1)],
                    start=(d == 0), stop=(d == 15)), "sT" if t == 1 else None)
            sqm[d] = b.c("sT")
        b.wait("vector", "sT", b.c("sT"))
        stats_rep(GRP2, HID, EPS_LN, True, bf_dst=reps)
        for d in range(16):
            b.op("vector", lambda e, d=d: e.tensor_mul(
                ring[:, d % 2, 0:T], xb[:, d, 0:T], reps[:, 0, 0:T]), "sV")
            b.op("vector", lambda e, d=d: e.tensor_sub(
                xb[:, d, 0:T], ring[:, d % 2, 0:T], reps[:, 1, 0:T]), "sV")
        xn2_done = b.c("sV")
        b.wait("tensor", "sV", xn2_done)
        for d in range(16):
            for t in range(2):
                b.op("tensor", lambda e, d=d, t=t: e.matmul(
                    psA[0:1, 5 + t, 0:512], onescb[:, :], xb[:, d, 512 * t:512 * (t + 1)],
                    start=(d == 0), stop=(d == 15)),
                    "sT" if (d == 15 and t == 1) else None)
        b.wait("vector", "sT", b.c("sT"))
        for t in range(2):
            b.op("vector", lambda e, t=t: e.tensor_scalar(
                corr[0:1, 0, 512 * t:512 * (t + 1)], psA[0:1, 5 + t, 0:512],
                -128.0, None, ALU.mult), "sV")
        corr1 = b.c("sV")

        wgR = wg.rearrange("(dc p) h -> p dc h", p=128)
        wuR = wu.rearrange("(dc p) h -> p dc h", p=128)
        fw_done = []
        fw_cons = []

        def load_fw_block(i):
            src = wgR if i % 2 == 0 else wuR
            ht = i // 2
            sem = "sW%d" % (i % 4)
            if i >= 4:
                b.wait("gpsimd", "sT", fw_cons[i - 4])
            b.op("gpsimd", lambda e, i=i, src=src, ht=ht: e.dma_start(
                out=wb1[:, 2048 * (i % 4):2048 * (i % 4) + 2048]
                .rearrange("p (dc h) -> p dc h", dc=16),
                in_=src[:, :, 128 * ht:128 * (ht + 1)]), sem)
            fw_done.append((sem, b.c(sem)))

        b.barrier(("gpsimd",))
        for i in range(4):
            load_fw_block(i)
        b.wait("scalar", "sV", corr1)
        b.wait("tensor", "sV", corr1)
        bank_free7 = {j: 0 for j in range(4)}
        pair_free = {0: 0, 1: 0}
        for ht in range(44):
            pr = ht % 2
            for gu_i in range(2):
                i = 2 * ht + gu_i
                b.wait("tensor", fw_done[i][0], fw_done[i][1])
                for t in range(2):
                    bank = (2 * i + t) % 4
                    b.wait("tensor", "sA", bank_free7[bank])
                    for d in range(16):
                        b.op("tensor", lambda e, i=i, d=d, t=t, bank=bank: e.matmul(
                            psA[:, bank, :],
                            wb1[:, 2048 * (i % 4) + 128 * d:2048 * (i % 4) + 128 * (d + 1)],
                            xb[:, d, 512 * t:512 * (t + 1)], start=(d == 0), stop=False), None)
                    b.op("tensor", lambda e, t=t, bank=bank: e.matmul(
                        psA[:, bank, :], ones1f[:, :], corr[0:1, 0, 512 * t:512 * (t + 1)],
                        start=False, stop=True), "sT")
                    b.wait("scalar", "sT", b.c("sT"))
                    if gu_i == 0 and t == 0:
                        b.wait("scalar", "sV", pair_free[pr])
                    if gu_i == 0:
                        b.op("scalar", lambda e, ht=ht, t=t, pr=pr, bank=bank: e.activation(
                            t64x4[:, 2 * pr, 512 * t:512 * (t + 1)], psA[:, bank, :], AF.Silu,
                            scale=sgS[:, ht:ht + 1]), "sA")
                    else:
                        b.op("scalar", lambda e, ht=ht, t=t, pr=pr, bank=bank: e.activation(
                            t64x4[:, 2 * pr + 1, 512 * t:512 * (t + 1)], psA[:, bank, :], AF.Copy,
                            scale=suS[:, ht:ht + 1]), "sA")
                    bank_free7[bank] = b.c("sA")
                fw_cons.append(b.c("sT"))
                if i + 4 < 88:
                    load_fw_block(i + 4)
            b.wait("vector", "sA", b.c("sA"))
            b.op("vector", lambda e, ht=ht, pr=pr: e.tensor_mul(
                hbc[:, ht, :], t64x4[:, 2 * pr, :], t64x4[:, 2 * pr + 1, :]), "sV")
            pair_free[pr] = b.c("sV")

        b.barrier()
        for d in range(44):
            for t in range(2):
                b.op("tensor", lambda e, d=d, t=t: e.matmul(
                    psA[0:1, 5 + t, 0:512], onescb[:, :], hbc[:, d, 512 * t:512 * (t + 1)],
                    start=(d == 0), stop=(d == 43)),
                    "sT" if (d == 43 and t == 1) else None)
        b.wait("vector", "sT", b.c("sT"))
        for t in range(2):
            b.op("vector", lambda e, t=t: e.tensor_scalar(
                corr[0:1, 1, 512 * t:512 * (t + 1)], psA[0:1, 5 + t, 0:512],
                -128.0, None, ALU.mult), "sV")
        corr2 = b.c("sV")

        wdR = wd.rearrange("(dc p) h -> p dc h", p=128)
        dw_done = []
        dw_cons = []
        wb1dw = wb1[:, 0:5632].rearrange("p (dc h) -> p dc h", dc=44)
        wb2dw = wb2.rearrange("p a b c -> p (a b c)")[:, 0:5632].rearrange(
            "p (dc h) -> p dc h", dc=44)

        def load_dw_block(i):
            sem = "sW%d" % (i % 2)
            dst = wb1dw if i % 2 == 0 else wb2dw
            if i >= 2:
                b.wait("gpsimd", "sT", dw_cons[i - 2])
            b.op("gpsimd", lambda e, i=i, dst=dst: e.dma_start(
                out=dst[:, :, :], in_=wdR[:, :, 128 * i:128 * (i + 1)]), sem)
            dw_done.append((sem, b.c(sem)))

        b.barrier(("gpsimd",))
        load_dw_block(0)
        load_dw_block(1)
        b.wait("scalar", "sV", corr2)
        b.wait("tensor", "sV", corr2)
        bank_free8 = {j: 0 for j in range(4)}
        for i in range(16):
            b.wait("tensor", dw_done[i][0], dw_done[i][1])
            wsl = wb1dw if i % 2 == 0 else wb2dw
            for t in range(2):
                bank = (2 * i + t) % 4
                b.wait("tensor", "sA", bank_free8[bank])
                for d in range(44):
                    b.op("tensor", lambda e, d=d, t=t, bank=bank, wsl=wsl: e.matmul(
                        psA[:, bank, :], wsl[:, d, :],
                        hbc[:, d, 512 * t:512 * (t + 1)], start=(d == 0), stop=False), None)
                b.op("tensor", lambda e, t=t, bank=bank: e.matmul(
                    psA[:, bank, :], ones1f[:, :], corr[0:1, 1, 512 * t:512 * (t + 1)],
                    start=False, stop=True), "sT")
                b.wait("scalar", "sT", b.c("sT"))
                b.op("scalar", lambda e, i=i, t=t, bank=bank: e.activation(
                    xb[:, i, 512 * t:512 * (t + 1)],
                    psA[:, bank, :], AF.Copy,
                    scale=sdS[:, i:i + 1]), "sA")
                bank_free8[bank] = b.c("sA")
            dw_cons.append(b.c("sT"))
            if i + 2 < 16:
                load_dw_block(i + 2)
        y2_done = b.c("sA")

        b.wait("vector", "sA", y2_done)
        b.wait("tensor", "sA", y2_done)
        sqm = {}
        for d in range(16):
            if d >= 2:
                b.wait("vector", "sT", sqm[d - 2])
            b.op("vector", lambda e, d=d: e.tensor_mul(
                ring[:, d % 2, 0:T], xb[:, d, 0:T], xb[:, d, 0:T]), "sV")
            b.wait("tensor", "sV", b.c("sV"))
            for t in range(2):
                b.op("tensor", lambda e, d=d, t=t: e.matmul(
                    psA[0:1, t, 0:512], onescb[:, :], xb[:, d, 512 * t:512 * (t + 1)],
                    start=(d == 0), stop=(d == 15)), None)
                b.op("tensor", lambda e, d=d, t=t: e.matmul(
                    psA[0:1, 2 + t, 0:512], onescb[:, :], ring[:, d % 2, 512 * t:512 * (t + 1)],
                    start=(d == 0), stop=(d == 15)), "sT" if t == 1 else None)
            sqm[d] = b.c("sT")
        b.wait("vector", "sT", b.c("sT"))
        stats_rep(GRP2, HID, EPS_LN, True, f_dst=repf)

        for d in range(16):
            b.op("sync", lambda e, d=d: e.dma_start(out=t64[:, d % 2, :], in_=x1D[d, :, :]), "sWs")
            xl = b.c("sWs")
            b.op("vector", lambda e, d=d: e.tensor_mul(
                cacc[:, d % 2, :], xb[:, d, 0:T], repf[:, 0, :]), "sV")
            b.op("vector", lambda e, d=d: e.tensor_sub(
                cacc[:, d % 2, :], cacc[:, d % 2, :], repf[:, 1, :]), "sV")
            b.wait("vector", "sWs", xl)
            b.op("vector", lambda e, d=d: e.tensor_add(
                t64[:, d % 2, :], t64[:, d % 2, :], cacc[:, d % 2, :]), "sV")
            ov = b.c("sV")
            b.wait("sync", "sV", ov)
            b.op("sync", lambda e, d=d: e.dma_start(
                out=outT[128 * d:128 * (d + 1), :], in_=t64[:, d % 2, :]), "sWs")
            b.wait("vector", "sWs", b.c("sWs"))

        b.barrier()

        @block.sync
        def _(e):
            for gfn in b.prog["sync"]:
                gfn(e, sems)

        @block.gpsimd
        def _(e):
            for gfn in b.prog["gpsimd"]:
                gfn(e, sems)

        @block.tensor
        def _(e):
            for gfn in b.prog["tensor"]:
                gfn(e, sems)

        @block.vector
        def _(e):
            for gfn in b.prog["vector"]:
                gfn(e, sems)

        @block.scalar
        def _(e):
            for gfn in b.prog["scalar"]:
                gfn(e, sems)

    return nc


_NC_CACHE = [None]


def kernel(**inp):
    f32 = np.float32
    x = np.asarray(inp["L0_attn_input_states"], dtype=f32)
    wipv = np.ascontiguousarray(np.asarray(inp["L0_attn_in_proj_w"], dtype=f32))
    wcvv = np.asarray(inp["L0_attn_conv_w"], dtype=f32)[:, 0, :]
    wcbv = np.asarray(inp["L0_attn_conv_b"], dtype=f32)
    dtbv = np.asarray(inp["L0_attn_dt_bias"], dtype=f32)
    alogv = np.asarray(inp["L0_attn_A_log"], dtype=f32)
    dv = np.asarray(inp["L0_attn_D"], dtype=f32)
    nwv = np.asarray(inp["L0_attn_norm_w"], dtype=f32)
    wopv = np.ascontiguousarray(np.asarray(inp["L0_attn_out_proj_w"], dtype=f32))
    wgq = np.ascontiguousarray(np.asarray(inp["L0_ffn_wq_gate"], dtype=np.int32).T)
    wuq = np.ascontiguousarray(np.asarray(inp["L0_ffn_wq_up"], dtype=np.int32).T)
    wdq = np.ascontiguousarray(np.asarray(inp["L0_ffn_wq_down"], dtype=np.int32).T)
    sgv = np.asarray(inp["L0_ffn_s_gate"], dtype=f32)
    suv = np.asarray(inp["L0_ffn_s_up"], dtype=f32)
    sdv = np.asarray(inp["L0_ffn_s_down"], dtype=f32)

    wcvS = np.ascontiguousarray(wcvv.reshape(4, 48, 128).transpose(2, 1, 0))
    wcbS = np.ascontiguousarray(wcbv.reshape(48, 128).T)
    nwS = np.ascontiguousarray(nwv.reshape(32, 128).T)
    sgS = np.ascontiguousarray(sgv.reshape(44, 128).T)
    suS = np.ascontiguousarray(suv.reshape(44, 128).T)
    sdS = np.ascontiguousarray(sdv.reshape(16, 128).T)
    dv2 = np.empty((128, 32), dtype=f32)
    dv2[0:64, :] = dv[0::2][None, :]
    dv2[64:128, :] = dv[1::2][None, :]

    if _NC_CACHE[0] is None:
        _NC_CACHE[0] = build_nc()
    nc = _NC_CACHE[0]
    in_maps = []
    for c in range(8):
        bb, half = c // 2, c % 2
        t0 = half * T
        xs = np.zeros((HID, TE), dtype=f32)
        lo, hi = t0 - 1, t0 + T + 2
        clo, chi = max(lo, 0), min(hi, 2048)
        xs[:, clo - lo:chi - lo] = x[bb].T[:, clo:chi]
        ws = np.zeros((128, 8), dtype=f32)
        if half == 1:
            ws[:, c - 1] = 1.0
        in_maps.append({
            "xT": xs, "wip": wipv, "wop": wopv, "wcv": wcvS, "wcb": wcbS,
            "dtb": dtbv.reshape(64, 1), "alog": alogv.reshape(64, 1),
            "dvec": dv.reshape(1, 64), "dvec2": dv2, "nw": nwS,
            "wg": wgq, "wu": wuq, "wd": wdq, "sg": sgS, "su": suS, "sdn": sdS,
            "wsel": ws,
        })
    res = run_bass_kernel_spmd(nc, in_maps, core_ids=list(range(8)))
    out = np.empty((4, 2048, HID), dtype=f32)
    for c in range(8):
        bb, half = c // 2, c % 2
        out[bb, half * T:(half + 1) * T, :] = res.results[c]["outT"].T
    return out


# version 14 (final)
# speedup vs baseline: 1.0889x; 1.0889x over previous
import numpy as np
from contextlib import ExitStack

import concourse.bass as bass
import concourse.mybir as mybir
from concourse.bass_utils import run_bass_kernel_spmd

F32 = mybir.dt.float32
BF16 = mybir.dt.bfloat16
I32 = mybir.dt.int32
U8 = mybir.dt.uint8
AF = mybir.ActivationFunctionType
ALU = mybir.AluOpType

HID = 2048
INTER = 4096
CH = 256
NCH = 4
T = 1024
TE = T + 3
PROJ = 10304
FH = 5632
EPS_LN = 1e-5
EPS_RMS = 1e-6

ENGS = ("sync", "gpsimd", "tensor", "vector", "scalar")


class B:
    def __init__(self):
        self.prog = {e: [] for e in ENGS}
        self.cnt = {}

    def op(self, eng, fn, inc=None, amt=1):
        if inc is not None and (inc.startswith("sW") or inc.startswith("sS")
                                or inc.startswith("sD") or inc.startswith("sE")
                                or inc in ("sX", "sG", "sZ")):
            amt = 16
        if inc is not None:
            self.cnt[inc] = self.cnt.get(inc, 0) + amt

            def g(e, sems, fn=fn, inc=inc, amt=amt):
                fn(e).then_inc(sems[inc], amt)
        else:
            def g(e, sems, fn=fn):
                fn(e)
        self.prog[eng].append(g)

    def ccop(self, eng, fn, inc):
        self.cnt[inc] = self.cnt.get(inc, 0) + 1

        def g(e, sems, fn=fn, inc=inc):
            fn(e).then_inc(sems[inc])
        self.prog[eng].append(g)

    def wait(self, eng, sem, val):
        if val <= 0:
            return

        def g(e, sems, sem=sem, val=val):
            e.wait_ge(sems[sem], val)
        self.prog[eng].append(g)

    def c(self, sem):
        return self.cnt.get(sem, 0)

    def barrier(self, engs=ENGS):
        snap = dict(self.cnt)
        for eng in engs:
            for s, v in snap.items():
                if v > 0 and s != "sCC":
                    self.wait(eng, s, v)


def build_nc():
    nc = bass.Bass("TRN2", target_bir_lowering=False, num_devices=8)

    xT = nc.dram_tensor("xT", [HID, TE], F32, kind="ExternalInput").ap()
    wip = nc.dram_tensor("wip", [HID, PROJ], F32, kind="ExternalInput").ap()
    wop = nc.dram_tensor("wop", [INTER, HID], F32, kind="ExternalInput").ap()
    wcv = nc.dram_tensor("wcv", [128, 48, 4], F32, kind="ExternalInput").ap()
    wcb = nc.dram_tensor("wcb", [128, 48], F32, kind="ExternalInput").ap()
    dtb = nc.dram_tensor("dtb", [64, 1], F32, kind="ExternalInput").ap()
    alog = nc.dram_tensor("alog", [64, 1], F32, kind="ExternalInput").ap()
    dvec = nc.dram_tensor("dvec", [1, 64], F32, kind="ExternalInput").ap()
    dvec2 = nc.dram_tensor("dvec2", [128, 32], F32, kind="ExternalInput").ap()
    nw = nc.dram_tensor("nw", [128, 32], F32, kind="ExternalInput").ap()
    wg = nc.dram_tensor("wg", [HID, FH], I32, kind="ExternalInput").ap()
    wu = nc.dram_tensor("wu", [HID, FH], I32, kind="ExternalInput").ap()
    wd = nc.dram_tensor("wd", [FH, HID], I32, kind="ExternalInput").ap()
    sg = nc.dram_tensor("sg", [128, 44], F32, kind="ExternalInput").ap()
    su = nc.dram_tensor("su", [128, 44], F32, kind="ExternalInput").ap()
    sdn = nc.dram_tensor("sdn", [128, 16], F32, kind="ExternalInput").ap()
    wsel = nc.dram_tensor("wsel", [128, 8], F32, kind="ExternalInput").ap()

    outT = nc.dram_tensor("outT", [HID, T], F32, kind="ExternalOutput").ap()

    acumD = nc.dram_tensor("acumD", [64, T], F32, kind="Internal").ap()
    acumeD = nc.dram_tensor("acumeD", [64, T], F32, kind="Internal").ap()
    dtD = nc.dram_tensor("dtD", [64, T], F32, kind="Internal").ap()
    dtdD = nc.dram_tensor("dtdD", [64, T], F32, kind="Internal").ap()
    aeD = nc.dram_tensor("aeD", [4, 64], F32, kind="Internal").ap()
    zDr = nc.dram_tensor("zDr", [NCH, 128, INTER], BF16, kind="Internal").ap()
    gateD = nc.dram_tensor("gateD", [32, 128, T], BF16, kind="Internal").ap()
    x1D = nc.dram_tensor("x1D", [16, 128, T], F32, kind="Internal").ap()
    ccI = nc.dram_tensor("ccI", [128, INTER], BF16, kind="Internal").ap()
    ccO = nc.dram_tensor("ccO", [8 * 128, INTER], BF16, kind="Internal").ap()

    es = ExitStack()
    with es:
        A = lambda *a: es.enter_context(nc.sbuf_tensor(*a))
        hbc = A("hbc", [128, 48, T], BF16)
        xb = A("xb", [128, 16, TE], BF16)
        wb1 = A("wb1", [128, 8192], BF16)
        uniB = A("uniB", [128, 29184], U8)
        uniC = A("uniC", [128, 16384], U8)
        uniR1 = A("uniR1", [128, 6144], U8)
        uniR2 = A("uniR2", [128, 4108], U8)
        ring = A("ring", [128, 2, 1028], BF16)
        idb = A("idb", [128, 128], BF16)
        idf = A("idf", [128, 128], F32)
        maskA = A("maskA", [128, 256], BF16)
        ones64 = A("ones64", [64, 256], BF16)
        ones1f = A("ones1f", [1, 128], F32)
        onescb = A("onescb", [128, 1], BF16)
        convw = A("convw", [128, 48, 4], F32)
        convb = A("convb", [128, 48], F32)
        dtbS = A("dtbS", [64, 1], F32)
        alogS = A("alogS", [64, 1], F32)
        eaS = A("eaS", [64, 1], F32)
        drep = A("drep", [128, 64], F32)
        drep2 = A("drep2", [128, 32], F32)
        nwS = A("nwS", [128, 32], F32)
        sgS = A("sgS", [128, 44], F32)
        suS = A("suS", [128, 44], F32)
        sdS = A("sdS", [128, 16], F32)
        wselS = A("wselS", [128, 8], F32)
        cstS = A("cstS", [1, 2], F32)

        def v(arena, off, nbytes, dt, shape, parts=128):
            ap = arena[0:parts, off:off + nbytes].bitcast(dt)
            if len(shape) > 1:
                kw = {}
                names = "abc"[:len(shape) - 1]
                pat = "p (" + " ".join(names) + ") -> p " + " ".join(names)
                for nm, s in zip(names, shape[1:]):
                    kw[nm] = s
                ap = ap.rearrange(pat, **kw)
            return ap

        convS = v(uniB, 0, 2 * TE * 4, F32, [128, 2, TE])
        rep8h = [v(uniB, 0, 4096, F32, [128, 4, 256]),
                 v(uniB, 4096, 4096, F32, [128, 4, 256])]
        wb2 = v(uniB, 0, 11264, BF16, [128, 2, 44, 64])
        dtf = v(uniB, 8320, 4096, F32, [64, 1024], parts=64)
        acum = v(uniB, 12544, 4096, F32, [64, 1024], parts=64)
        scr64 = v(uniB, 16704, 4096, F32, [64, 1024], parts=64)

        corr = v(uniB, 12544, 8192, F32, [1, 2, 1024], parts=1)
        cacc = v(uniB, 20864, 8192, F32, [128, 2, 1024])
        dsb = v(uniB, 20864, 6144, F32, [128, 4, 384])
        zbuf = v(uniB, 20864, 8192, BF16, [128, 4096])
        msb = v(uniB, 8320, 3072, BF16, [128, 4, 384])
        dtr2 = v(uniB, 8320, 1024, BF16, [128, 2, 256])
        maskB = maskA[:, 0:128]

        s_cur = v(uniC, 0, 8192, BF16, [128, 4096])
        s_nxt = v(uniC, 8192, 8192, BF16, [128, 4096])
        t64 = v(uniC, 0, 8192, F32, [128, 2, 1024])
        t64x4 = v(uniC, 0, 16384, F32, [128, 4, 1024])
        repf = v(uniC, 8192, 8192, F32, [128, 2, 1024])

        rows3 = v(uniR1, 0, 6144, F32, [1, 3, 512], parts=1)
        ttile = v(uniR1, 0, 1024, BF16, [128, 4, 128])
        zrow = v(uniR1, 1024, 2048, BF16, [128, 2, 512])
        acumT = v(uniR1, 3072, 2048, F32, [128, 8, 64])
        aes = v(uniR1, 5120, 1024, F32, [128, 256])
        yscr = v(uniR1, 1024, 1024, F32, [64, 256], parts=64)
        yscrH = uniR1[64:128, 1024:2048].bitcast(F32)
        ysF = uniR1[0:128, 1024:2048].bitcast(F32)

        reps = v(uniR2, 0, 2 * TE * 2, BF16, [128, 2, TE])
        btile = v(uniR2, 0, 4096, BF16, [128, 8, 2, 128])
        sgr = v(uniB, 0, 4096, BF16, [128, 2, 2, 512])
        esb = v(uniR2, 0, 3072, BF16, [128, 4, 384])

        psA = es.enter_context(nc.psum_tensor("psA", [128, 7, 512], F32))
        psT = es.enter_context(nc.psum_tensor("psT", [128, 8, 128], BF16))

        sems = {}
        for s in ("sWs", "sWg", "sT", "sV", "sA", "sP", "sCC",
                  "sX", "sG", "sZ", "sW0", "sW1", "sW2", "sW3", "sS0", "sS1",
                  "sD0", "sD1", "sE0", "sE1"):
            sems[s] = es.enter_context(nc.semaphore(s))
        block = es.enter_context(nc.Block())
        b = B()

        for buf in (idb, idf):
            b.op("gpsimd", lambda e, t=buf: e.memset(t[:, :], 0.0), "sP")
            b.op("gpsimd", lambda e, t=buf: e.affine_select(
                out=t[:, :], in_=t[:, :], compare_op=ALU.not_equal, fill=1.0,
                base=0, pattern=[[-1, 128]], channel_multiplier=1), "sP")
        for buf, n in ((maskA, 256),):
            b.op("gpsimd", lambda e, t=buf: e.memset(t[:, :], 0.0), "sP")
            b.op("gpsimd", lambda e, t=buf, n=n: e.affine_select(
                out=t[:, :], in_=t[:, :], compare_op=ALU.is_ge, fill=-1e30,
                base=0, pattern=[[1, n]], channel_multiplier=-1), "sP")
        b.op("gpsimd", lambda e: e.memset(cstS[0:1, 0:1], EPS_LN), "sP")
        b.op("gpsimd", lambda e: e.memset(cstS[0:1, 1:2], EPS_RMS), "sP")
        nc.const_aps.register(F32, EPS_LN, cstS[0:1, 0:1])
        nc.const_aps.register(F32, EPS_RMS, cstS[0:1, 1:2])
        for buf in (ones64, ones1f, onescb):
            b.op("gpsimd", lambda e, t=buf: e.memset(
                t[tuple(slice(None) for _ in t.shape)], 1.0), "sP")
        consts_done = b.c("sP")

        for dst, src in ((convw, wcv), (convb, wcb), (dtbS, dtb), (alogS, alog),
                         (nwS, nw), (sgS, sg), (suS, su), (sdS, sdn), (wselS, wsel)):
            b.op("sync", lambda e, d=dst, s=src: e.dma_start(
                out=d[tuple(slice(None) for _ in d.shape)], in_=s), "sWs")
        b.op("sync", lambda e: e.dma_start(
            out=drep[:, :], in_=dvec.broadcast_to([128, 64])), "sWs")
        b.op("sync", lambda e: e.dma_start(out=drep2[:, :], in_=dvec2), "sWs")
        ld_small = b.c("sWs")
        b.wait("scalar", "sWs", ld_small)
        b.wait("scalar", "sP", b.c("sP"))
        b.op("scalar", lambda e: e.activation(eaS[:, :], alogS[:, :], AF.Exp), "sA")

        for d in range(16):
            b.op("gpsimd", lambda e, d=d: e.dma_start(
                out=xb[:, d, :], in_=xT[128 * d:128 * (d + 1), :]), "sX")
        ld_x = b.c("sX")

        IP_BLOCKS = [(4096 + 256 * i, 256) for i in range(24)]
        IP_BLOCKS.append((10240, 64))
        IP_BLOCKS += [(256 * i, 256) for i in range(16)]
        wipR = wip.rearrange("(dc p) h -> p dc h", p=128)
        wb1v = wb1.rearrange("p (a b c) -> p a b c", a=2, b=16)
        ipw_done = []
        ipw_cons = []

        def load_ip_block(i):
            c0, w = IP_BLOCKS[i]
            sem = "sW%d" % (i % 2)
            if i >= 2:
                b.wait("gpsimd", "sT", ipw_cons[i - 2])
            b.op("gpsimd", lambda e, c0=c0, w=w, i=i: e.dma_start(
                out=wb1v[:, i % 2, :, 0:w], in_=wipR[:, :, c0:c0 + w]), sem)
            ipw_done.append((sem, b.c(sem)))

        load_ip_block(0)
        load_ip_block(1)

        def stats_rep(grp, nfeat, eps, with_mean, bf_dst=None, f_dst=None):
            ng = len(grp)
            for gi, (o, w) in enumerate(grp):
                if with_mean:
                    b.op("vector", lambda e, gi=gi, w=w: e.tensor_scalar(
                        psA[0:1, gi, 0:w], psA[0:1, gi, 0:w], 1.0 / nfeat, None,
                        ALU.mult), "sV")
                b.op("vector", lambda e, gi=gi, w=w, ng=ng: e.tensor_scalar(
                    psA[0:1, ng + gi, 0:w], psA[0:1, ng + gi, 0:w], 1.0 / nfeat, None,
                    ALU.mult), "sV")
                if with_mean:
                    b.wait("scalar", "sV", b.c("sV"))
                    b.op("scalar", lambda e, gi=gi, w=w: e.activation(
                        rows3[0:1, 2, 0:w], psA[0:1, gi, 0:w], AF.Square), "sA")
                    b.wait("vector", "sA", b.c("sA"))
                    b.op("vector", lambda e, gi=gi, w=w, ng=ng: e.tensor_sub(
                        psA[0:1, ng + gi, 0:w], psA[0:1, ng + gi, 0:w],
                        rows3[0:1, 2, 0:w]), "sV")
                b.wait("scalar", "sV", b.c("sV"))
                b.op("scalar", lambda e, gi=gi, w=w, ng=ng, eps=eps: e.activation(
                    rows3[0:1, 2, 0:w], psA[0:1, ng + gi, 0:w], AF.Ln, bias=eps), "sA")
                b.op("scalar", lambda e, w=w: e.activation(
                    rows3[0:1, 0, 0:w], rows3[0:1, 2, 0:w], AF.Exp, scale=-0.5), "sA")
                if with_mean:
                    b.wait("vector", "sA", b.c("sA"))
                    b.op("vector", lambda e, gi=gi, w=w: e.tensor_mul(
                        rows3[0:1, 1, 0:w], psA[0:1, gi, 0:w], rows3[0:1, 0, 0:w]), "sV")
                nrep = 2 if with_mean else 1
                for ri in range(nrep):
                    b.wait("tensor", "sA" if ri == 0 else "sV",
                           b.c("sA") if ri == 0 else b.c("sV"))
                    b.op("tensor", lambda e, w=w, ri=ri: e.matmul(
                        psA[:, 6, 0:w], ones1f[:, :], rows3[0:1, ri, 0:w],
                        start=True, stop=True), "sT")
                    b.wait("vector", "sT", b.c("sT"))
                    if bf_dst is not None:
                        b.op("vector", lambda e, o=o, w=w, ri=ri, dd=bf_dst: e.tensor_copy(
                            dd[:, ri, o:o + w], psA[:, 6, 0:w]), "sV")
                    if f_dst is not None:
                        b.op("vector", lambda e, o=o, w=w, ri=ri, dd=f_dst: e.tensor_copy(
                            dd[:, ri, o:o + w], psA[:, 6, 0:w]), "sV")
                    b.wait("tensor", "sV", b.c("sV"))

        GRP3 = ((0, 512), (512, 512), (1024, 3))
        GRP2 = ((0, 512), (512, 512))
        b.wait("tensor", "sP", consts_done)
        b.wait("tensor", "sX", ld_x)
        b.wait("vector", "sX", ld_x)
        sqm = {}
        for d in range(16):
            if d >= 2:
                b.wait("vector", "sT", sqm[d - 2])
            b.op("vector", lambda e, d=d: e.tensor_mul(
                ring[:, d % 2, 0:TE], xb[:, d, :], xb[:, d, :]), "sV")
            b.wait("tensor", "sV", b.c("sV"))
            for gi, (o, w) in enumerate(GRP3):
                b.op("tensor", lambda e, d=d, gi=gi, o=o, w=w: e.matmul(
                    psA[0:1, gi, 0:w], onescb[:, :], xb[:, d, o:o + w],
                    start=(d == 0), stop=(d == 15)), None)
            for gi, (o, w) in enumerate(GRP3):
                b.op("tensor", lambda e, d=d, gi=gi, o=o, w=w: e.matmul(
                    psA[0:1, 3 + gi, 0:w], onescb[:, :], ring[:, d % 2, o:o + w],
                    start=(d == 0), stop=(d == 15)), "sT" if gi == 2 else None)
            sqm[d] = b.c("sT")
        b.wait("vector", "sT", b.c("sT"))
        stats_rep(GRP3, HID, EPS_LN, True, bf_dst=reps)
        for d in range(16):
            b.op("vector", lambda e, d=d: e.tensor_mul(
                ring[:, d % 2, 0:TE], xb[:, d, :], reps[:, 0, :]), "sV")
            b.op("vector", lambda e, d=d: e.tensor_sub(
                xb[:, d, :], ring[:, d % 2, 0:TE], reps[:, 1, :]), "sV")
        xn1_done = b.c("sV")

        b.wait("tensor", "sV", xn1_done)
        silu_cnt = {}
        bank_free = {j: 0 for j in range(6)}
        gcount = 0
        for i in range(24):
            b.wait("tensor", ipw_done[i][0], ipw_done[i][1])
            for ht in range(2):
                k = 2 * i + ht
                for gi, (o, w) in enumerate(GRP3):
                    bank = gcount % 6
                    gcount += 1
                    b.wait("tensor", "sV", bank_free[bank])
                    for d in range(16):
                        b.op("tensor", lambda e, d=d, o=o, w=w, i=i, ht=ht, bank=bank: e.matmul(
                            psA[:, bank, 0:w], wb1v[:, i % 2, d, 128 * ht:128 * (ht + 1)],
                            xb[:, d, o:o + w], start=(d == 0), stop=(d == 15)),
                            "sT" if d == 15 else None)
                    b.wait("vector", "sT", b.c("sT"))
                    b.op("vector", lambda e, o=o, w=w, k=k, bank=bank: e.tensor_copy(
                        convS[:, k % 2, o:o + w], psA[:, bank, 0:w]), "sV")
                    bank_free[bank] = b.c("sV")
                if k >= 1:
                    b.wait("vector", "sA", silu_cnt[k - 1])
                b.op("vector", lambda e, k=k: e.tensor_scalar(
                    cacc[:, 0, :], convS[:, k % 2, 0:T], convw[:, k, 0:1], None, ALU.mult), "sV")
                b.op("vector", lambda e, k=k: e.scalar_tensor_tensor(
                    cacc[:, 1, :], convS[:, k % 2, 1:1 + T], convw[:, k, 1:2],
                    cacc[:, 0, :], ALU.mult, ALU.add), "sV")
                b.op("vector", lambda e, k=k: e.scalar_tensor_tensor(
                    cacc[:, 0, :], convS[:, k % 2, 2:2 + T], convw[:, k, 2:3],
                    cacc[:, 1, :], ALU.mult, ALU.add), "sV")
                b.op("vector", lambda e, k=k: e.scalar_tensor_tensor(
                    cacc[:, 1, :], convS[:, k % 2, 3:3 + T], convw[:, k, 3:4],
                    cacc[:, 0, :], ALU.mult, ALU.add), "sV")
                b.wait("scalar", "sV", b.c("sV"))
                b.op("scalar", lambda e, k=k: e.activation(
                    hbc[:, k, :], cacc[:, 1, :], AF.Silu, bias=convb[:, k:k + 1]), "sA")
                silu_cnt[k] = b.c("sA")
            ipw_cons.append(b.c("sT"))
            if i + 2 < len(IP_BLOCKS):
                load_ip_block(i + 2)

        b.wait("tensor", ipw_done[24][0], ipw_done[24][1])
        b.wait("tensor", "sV", bank_free[0])
        b.wait("scalar", "sWs", ld_small)
        for t in range(2):
            for d in range(16):
                b.op("tensor", lambda e, d=d, t=t: e.matmul(
                    psA[0:64, 0, 0:512], wb1v[:, 0, d, 0:64],
                    xb[:, d, 1 + 512 * t:1 + 512 * (t + 1)],
                    start=(d == 0), stop=(d == 15)), "sT" if d == 15 else None)
            b.wait("scalar", "sT", b.c("sT"))
            b.op("scalar", lambda e, t=t: e.activation(
                scr64[:, 512 * t:512 * (t + 1)], psA[0:64, 0, 0:512], AF.Exp,
                bias=dtbS[:, 0:1]), "sA")
            b.wait("tensor", "sA", b.c("sA"))
        ipw_cons.append(b.c("sT"))
        load_ip_block(26)
        b.op("scalar", lambda e: e.activation(dtf[:, :], scr64[:, :], AF.Ln, bias=1.0), "sA")
        b.wait("vector", "sA", b.c("sA"))
        b.op("vector", lambda e: e.tensor_scalar_max(dtf[:, :], dtf[:, :], 0.0), "sV")
        b.op("vector", lambda e: e.tensor_scalar(
            scr64[:, :], dtf[:, :], eaS[:, 0:1], -1.0, ALU.mult, ALU.mult), "sV")
        for c in range(NCH):
            b.op("vector", lambda e, c=c: e.tensor_tensor_scan(
                acum[:, CH * c:CH * (c + 1)], ones64[:, :], scr64[:, CH * c:CH * (c + 1)],
                0.0, ALU.mult, ALU.add), "sV")
        for c in range(NCH):
            b.op("vector", lambda e, c=c: e.tensor_scalar(
                scr64[:, CH * c:CH * (c + 1)], acum[:, CH * c:CH * (c + 1)],
                acum[:, CH * (c + 1) - 1:CH * (c + 1)], -1.0, ALU.subtract, ALU.mult), "sV")
        b.wait("scalar", "sV", b.c("sV"))
        b.op("scalar", lambda e: e.activation(scr64[:, :], scr64[:, :], AF.Exp), "sA")
        b.wait("vector", "sA", b.c("sA"))
        b.op("vector", lambda e: e.tensor_mul(scr64[:, :], scr64[:, :], dtf[:, :]), "sV")
        dtdec_done = b.c("sV")

        b.wait("sync", "sV", dtdec_done)
        b.op("sync", lambda e: e.dma_start(
            out=dtD, in_=dtf[:, :]), "sWs")
        b.op("sync", lambda e: e.dma_start(
            out=dtdD, in_=scr64[:, :]), "sWs")
        b.op("sync", lambda e: e.dma_start(out=acumD, in_=acum[:, :]), "sWs")
        b.op("sync", lambda e: e.dma_start(
            out=aeD.transpose([1, 0]), in_=acum[:, CH - 1::CH],
            allow_slow_non_contiguous=True), "sWs")
        b.op("scalar", lambda e: e.activation(cacc[0:64, 0, :], acum[:, :], AF.Exp), "sA")
        b.wait("sync", "sA", b.c("sA"))
        b.op("sync", lambda e: e.dma_start(out=acumeD, in_=cacc[0:64, 0, :]), "sWs")
        dt_out = b.c("sWs")

        b.wait("tensor", "sV", dtdec_done)
        for cs in range(8):
            b.op("tensor", lambda e, cs=cs: e.transpose(
                psA[:, 4, 0:64], acum[:, 128 * cs:128 * (cs + 1)], idf[0:64, 0:64]), "sT")
            b.wait("vector", "sT", b.c("sT"))
            b.op("vector", lambda e, cs=cs: e.tensor_copy(
                acumT[:, cs, :], psA[:, 4, 0:64]), "sV")
            b.wait("tensor", "sV", b.c("sV"))
        b.wait("sync", "sWs", dt_out)
        b.op("sync", lambda e: e.dma_start(
            out=aes[:, :],
            in_=aeD.rearrange("c h -> (c h)").unsqueeze(0).broadcast_to([128, 256])), "sWs")
        b.wait("scalar", "sWs", b.c("sWs"))
        b.op("scalar", lambda e: e.activation(aes[:, :], aes[:, :], AF.Exp), "sA")

        b.barrier()
        zrow_free = {0: 0, 1: 0}
        dtr2_free = {0: 0, 1: 0}
        dtr2_done = {}
        def s3_units():

            def load_dtr2(c2, k2):
                slot = k2 % 2
                sem = "sD%d" % slot
                b.wait("gpsimd", "sV", dtr2_free[slot])
                b.op("gpsimd", lambda e, c2=c2, k2=k2, slot=slot: e.dma_start(
                    out=dtr2[0:64, slot, :],
                    in_=dtdD[2 * k2, CH * c2:CH * (c2 + 1)].unsqueeze(0)
                    .broadcast_to([64, CH])), sem)
                b.op("gpsimd", lambda e, c2=c2, k2=k2, slot=slot: e.dma_start(
                    out=dtr2[64:128, slot, :],
                    in_=dtdD[2 * k2 + 1, CH * c2:CH * (c2 + 1)].unsqueeze(0)
                    .broadcast_to([64, CH])), sem)
                dtr2_done[slot] = (sem, b.c(sem))

            for c in range(NCH):
                for g in range(8):
                    for lh in range(2):
                        b.op("tensor", lambda e, c=c, g=g, lh=lh: e.transpose(
                            psT[:, (2 * g + lh) % 4, :],
                            hbc[:, 32 + g, CH * c + 128 * lh:CH * c + 128 * (lh + 1)],
                            idb[:, :]), "sT")
                    if g % 2 == 1:
                        b.wait("vector", "sT", b.c("sT"))
                        b.op("vector", lambda e, g=g: e.tensor_copy(
                            btile[:, g - 1:g + 1, :, :].rearrange("p a b c -> p (a b c)"),
                            psT[:, 0:4, :].rearrange("p a b -> p (a b)")), "sV")
                        b.wait("tensor", "sV", b.c("sV"))
                        yield
                load_dtr2(c, 0)
                load_dtr2(c, 1)
                for g in range(8):
                    zbank = 5 + (g % 2)
                    for kk in range(4):
                        k = 4 * g + kk
                        b.wait("vector", dtr2_done[k % 2][0], dtr2_done[k % 2][1])
                        b.op("vector", lambda e, c=c, k=k: e.tensor_mul(
                            ring[:, k % 2, 0:CH], hbc[:, k, CH * c:CH * (c + 1)],
                            dtr2[:, k % 2, :]), "sV")
                        dtr2_free[k % 2] = b.c("sV")
                        if k + 2 < 32:
                            load_dtr2(c, k + 2)
                        b.wait("tensor", "sV", b.c("sV"))
                        yield
                        for lh in range(2):
                            b.op("tensor", lambda e, k=k, lh=lh: e.transpose(
                                psT[:, 4 + 2 * (k % 2) + lh, :],
                                ring[:, k % 2, 128 * lh:128 * (lh + 1)], idb[:, :]), "sT")
                        b.wait("vector", "sT", b.c("sT"))
                        b.op("vector", lambda e, k=k: e.tensor_copy(
                            ttile[:, 2 * (k % 2):2 * (k % 2) + 2, :].rearrange("p a b -> p (a b)"),
                            psT[:, 4 + 2 * (k % 2):4 + 2 * (k % 2) + 2, :]
                            .rearrange("p a b -> p (a b)")), "sV")
                        b.wait("tensor", "sV", b.c("sV"))
                        for hh in range(2):
                            h = 2 * k + hh
                            for lh in range(2):
                                b.op("tensor", lambda e, k=k, hh=hh, lh=lh, h=h, zbank=zbank: e.matmul(
                                    psA[:, zbank, (h % 8) * 64:(h % 8) * 64 + 64],
                                    btile[:, h // 8, lh, :],
                                    ttile[:, 2 * (k % 2) + lh, 64 * hh:64 * hh + 64],
                                    start=(lh == 0), stop=(lh == 1)),
                                    "sT" if lh == 1 else None)
                    zg = b.c("sT")
                    b.wait("vector", "sT", zg)
                    b.wait("vector", "sP", zrow_free[g % 2])
                    src = s_cur if c % 2 == 1 else s_nxt
                    dst = s_nxt if c % 2 == 1 else s_cur
                    b.op("vector", lambda e, g=g, zbank=zbank: e.tensor_copy(
                        zrow[:, g % 2, :], psA[:, zbank, :]), "sV")
                    zev = b.c("sV")
                    b.wait("gpsimd", "sV", zev)
                    if c == 0:
                        b.op("gpsimd", lambda e, g=g, dst=dst: e.tensor_copy(
                            dst[:, 512 * g:512 * (g + 1)], zrow[:, g % 2, :]), "sP")
                    else:
                        b.op("gpsimd", lambda e, c=c, g=g, src=src, dst=dst: e.tensor_tensor(
                            dst[:, 512 * g:512 * (g + 1)].rearrange("p (h j) -> p h j", h=8),
                            src[:, 512 * g:512 * (g + 1)].rearrange("p (h j) -> p h j", h=8),
                            aes[:, 64 * c + 8 * g:64 * c + 8 * g + 8].unsqueeze(2)
                            .broadcast_to([128, 8, 64]), ALU.mult), "sP")
                        b.op("gpsimd", lambda e, g=g, dst=dst: e.tensor_tensor(
                            dst[:, 512 * g:512 * (g + 1)], dst[:, 512 * g:512 * (g + 1)],
                            zrow[:, g % 2, :], ALU.add), "sP")
                    zrow_free[g % 2] = b.c("sP")
                    b.wait("sync", "sV", zev)
                    b.op("sync", lambda e, c=c, g=g: e.dma_start(
                        out=zDr[c, :, 512 * g:512 * (g + 1)], in_=zrow[:, g % 2, :]), "sWs")
                    b.wait("vector", "sWs", b.c("sWs"))
                    yield
            zs_done = b.c("sP")


        _s3g = s3_units()

        def s3_drain(n):
            for _ in range(n):
                if next(_s3g, "END") == "END":
                    break

        bank_free4 = {j: 0 for j in range(4)}
        sgr_free = {0: ("sS0", 0), 1: ("sS1", 0)}
        for i in range(25, 41):
            b.wait("tensor", ipw_done[i][0], ipw_done[i][1])
            for ht in range(2):
                gt = (IP_BLOCKS[i][0] // 128) + ht
                sl = gt % 2
                for t in range(2):
                    bank = (2 * gt + t) % 4
                    b.wait("tensor", "sA", bank_free4[bank])
                    for d in range(16):
                        b.op("tensor", lambda e, d=d, i=i, ht=ht, t=t, bank=bank: e.matmul(
                            psA[:, bank, :], wb1v[:, i % 2, d, 128 * ht:128 * (ht + 1)],
                            xb[:, d, 1 + 512 * t:1 + 512 * (t + 1)],
                            start=(d == 0), stop=(d == 15)), "sT" if d == 15 else None)
                    b.wait("scalar", "sT", b.c("sT"))
                    if t == 0:
                        b.wait("scalar", sgr_free[sl][0], sgr_free[sl][1])
                    b.op("scalar", lambda e, t=t, sl=sl, bank=bank: e.activation(
                        sgr[:, sl, t, :], psA[:, bank, :], AF.Silu), "sA")
                    bank_free4[bank] = b.c("sA")
                b.wait("sync", "sA", b.c("sA"))
                ssem = "sS%d" % sl
                b.op("sync", lambda e, gt=gt, sl=sl: e.dma_start(
                    out=gateD[gt, :, :],
                    in_=sgr[:, sl, :, :].rearrange("p a b -> p (a b)")), ssem)
                sgr_free[sl] = (ssem, b.c(ssem))
            ipw_cons.append(b.c("sT"))
            if i + 2 < len(IP_BLOCKS):
                load_ip_block(i + 2)
            s3_drain(11)
        s3_drain(1000)
        s_fin = s_nxt
        b.wait("gpsimd", "sV", b.c("sV"))
        b.op("gpsimd", lambda e: e.dma_start(out=ccI, in_=s_fin[:, :]), "sWg")
        b.wait("gpsimd", "sWg", b.c("sWg"))
        b.ccop("gpsimd", lambda e: e.collective_compute(
            "AllGather", ALU.bypass, replica_groups=[list(range(8))],
            ins=[ccI.opt()], outs=[ccO.opt()]), "sCC")
        b.wait("gpsimd", "sCC", b.c("sCC"))
        b.wait("vector", "sP", b.c("sP"))
        for r in range(8):
            b.op("gpsimd", lambda e, r=r: e.dma_start(
                out=zbuf[:, :], in_=ccO[128 * r:128 * (r + 1), :]), "sZ")
            b.wait("vector", "sZ", b.c("sZ"))
            if r == 0:
                b.op("vector", lambda e: e.tensor_scalar(
                    s_cur[:, :], zbuf[:, :], wselS[:, 0:1], None, ALU.mult), "sV")
            else:
                src, dst = (s_cur, s_nxt) if r % 2 == 1 else (s_nxt, s_cur)
                b.op("vector", lambda e, r=r, src=src, dst=dst: e.scalar_tensor_tensor(
                    dst[:, :], zbuf[:, :], wselS[:, r:r + 1], src[:, :],
                    ALU.mult, ALU.add), "sV")
            b.wait("gpsimd", "sV", b.c("sV"))
        s_start0 = s_nxt
        s_other0 = s_cur


        b.barrier()
        wopR = wop.rearrange("(dc p) h -> p dc h", p=128)
        opw_done = []
        opw_cons = []

        def load_op_block(i):
            sem = "sW%d" % (i % 2)
            if i >= 2:
                b.wait("gpsimd", "sT", opw_cons[i - 2])
            b.op("gpsimd", lambda e, i=i: e.dma_start(
                out=wb1[:, 4096 * (i % 2):4096 * (i % 2) + 4096]
                .rearrange("p (dc h) -> p dc h", dc=32),
                in_=wopR[:, :, 128 * i:128 * (i + 1)]), sem)
            opw_done.append((sem, b.c(sem)))

        load_op_block(0)
        load_op_block(1)

        xbf = xb.rearrange("p a b -> p (a b)")
        dtrt = xbf[:, 1024:1536].rearrange("p (s t) -> p s t", s=2)
        ert = v(uniB, 12544, 2048, F32, [128, 2, 256])
        gatec = xbf[:, 8192:16384].rearrange("p (k t) -> p k t", k=32)

        s_c = s_start0
        s_o = s_other0
        prev_chunk_sV = 0
        ring_free = {0: 0, 1: 0}
        halfq = [(c, g, hf) for c in range(NCH) for g in range(8) for hf in range(2)]
        hload_done = {}
        half_free = {0: 0, 1: 0}

        def load_half(hidx):
            hc, hg, hf = halfq[hidx]
            slot = hidx % 2
            sem = "sD%d" % slot
            hb = 8 * hg + 4 * hf
            b.wait("sync", "sV", half_free[slot])
            b.op("sync", lambda e, hc=hc, hb=hb, slot=slot: e.dma_start(
                out=rep8h[slot][:, :, :],
                in_=acumD[hb:hb + 4, CH * hc:CH * (hc + 1)]
                .unsqueeze(0).broadcast_to([128, 4, CH])), sem)
            hload_done[hidx] = (sem, b.c(sem))

        load_half(0)
        load_half(1)
        dtrt_done = {}

        def load_dtrt(c2, k2):
            slot = k2 % 2
            sem = "sE%d" % slot
            b.op("gpsimd", lambda e, c2=c2, k2=k2, slot=slot: e.dma_start(
                out=dtrt[0:64, slot, :],
                in_=dtD[2 * k2, CH * c2:CH * (c2 + 1)].unsqueeze(0)
                .broadcast_to([64, CH])), sem)
            b.op("gpsimd", lambda e, c2=c2, k2=k2, slot=slot: e.dma_start(
                out=dtrt[64:128, slot, :],
                in_=dtD[2 * k2 + 1, CH * c2:CH * (c2 + 1)].unsqueeze(0)
                .broadcast_to([64, CH])), sem)
            b.op("gpsimd", lambda e, c2=c2, k2=k2, slot=slot: e.dma_start(
                out=ert[0:64, slot, :],
                in_=acumeD[2 * k2, CH * c2:CH * (c2 + 1)].unsqueeze(0)
                .broadcast_to([64, CH])), sem)
            b.op("gpsimd", lambda e, c2=c2, k2=k2, slot=slot: e.dma_start(
                out=ert[64:128, slot, :],
                in_=acumeD[2 * k2 + 1, CH * c2:CH * (c2 + 1)].unsqueeze(0)
                .broadcast_to([64, CH])), sem)
            dtrt_done[slot] = (sem, b.c(sem))

        for c in range(NCH):
            b.wait("sync", "sV", prev_chunk_sV)
            b.op("sync", lambda e, c=c: e.dma_start(
                out=gatec[:, :, :],
                in_=gateD[:, :, CH * c:CH * (c + 1)].rearrange("k p t -> p k t")), "sG")
            gcl = b.c("sG")
            b.wait("gpsimd", "sV", prev_chunk_sV)
            load_dtrt(c, 0)
            load_dtrt(c, 1)
            if c > 0:
                b.op("gpsimd", lambda e, c=c: e.dma_start(
                    out=zbuf[:, :], in_=zDr[c - 1, :, :]), "sZ")
                b.wait("gpsimd", "sZ", b.c("sZ"))
                b.op("gpsimd", lambda e, c=c, src=s_c, dst=s_o: e.tensor_tensor(
                    dst[:, :].rearrange("p (h j) -> p h j", h=64),
                    src[:, :].rearrange("p (h j) -> p h j", h=64),
                    aes[:, 64 * (c - 1):64 * c].unsqueeze(2).broadcast_to([128, 64, 64]),
                    ALU.mult), "sP")
                b.op("gpsimd", lambda e, dst=s_o: e.tensor_tensor(
                    dst[:, :], dst[:, :], zbuf[:, :], ALU.add), "sP")
                s_c, s_o = s_o, s_c
            schain_p = b.c("sP")
            b.wait("tensor", "sP", schain_p)
            b.wait("vector", "sP", schain_p)
            b.wait("vector", "sG", gcl)

            for g in range(8):
                gbank = 4 + (g % 2)
                b.op("tensor", lambda e, c=c, g=g, gbank=gbank: e.matmul(
                    psA[:, gbank, 0:256], hbc[:, 32 + g, CH * c:CH * c + 128],
                    hbc[:, 40 + g, CH * c:CH * (c + 1)], start=True, stop=True), None)
                b.op("tensor", lambda e, c=c, g=g, gbank=gbank: e.matmul(
                    psA[:, gbank, 256:384], hbc[:, 32 + g, CH * c + 128:CH * (c + 1)],
                    hbc[:, 40 + g, CH * c + 128:CH * (c + 1)], start=True, stop=True), "sT")
                gmm = b.c("sT")
                for hf in range(2):
                    hidx = (c * 8 + g) * 2 + hf
                    slot = hidx % 2
                    if hf == 1:
                        b.wait("vector", "sA", b.c("sA"))
                        b.wait("vector", "sT", b.c("sT"))
                    b.wait("vector", hload_done[hidx][0], hload_done[hidx][1])
                    hb = 8 * g + 4 * hf
                    b.op("vector", lambda e, c=c, hb=hb, slot=slot: e.tensor_tensor(
                        dsb[:, :, 0:256], rep8h[slot][:, :, :],
                        acumT[:, 2 * c, hb:hb + 4].unsqueeze(2).broadcast_to([128, 4, CH]),
                        ALU.subtract), "sV")
                    b.op("vector", lambda e: e.tensor_tensor(
                        dsb[:, :, 0:256], dsb[:, :, 0:256],
                        maskA[:, :].unsqueeze(1).broadcast_to([128, 4, CH]), ALU.min), "sV")
                    b.op("vector", lambda e, c=c, hb=hb, slot=slot: e.tensor_tensor(
                        dsb[:, :, 256:384], rep8h[slot][:, :, 128:256],
                        acumT[:, 2 * c + 1, hb:hb + 4].unsqueeze(2).broadcast_to([128, 4, 128]),
                        ALU.subtract), "sV")
                    b.op("vector", lambda e: e.tensor_tensor(
                        dsb[:, :, 256:384], dsb[:, :, 256:384],
                        maskA[:, 0:128].unsqueeze(1).broadcast_to([128, 4, 128]), ALU.min), "sV")
                    dvd = b.c("sV")
                    b.wait("scalar", "sV", dvd)
                    b.op("scalar", lambda e: e.activation(
                        esb.rearrange("p a b -> p (a b)"),
                        dsb.rearrange("p a b -> p (a b)"), AF.Exp), "sA")
                    eexp = b.c("sA")
                    b.wait("vector", "sA", eexp)
                    b.wait("vector", "sT", gmm)
                    b.op("vector", lambda e, gbank=gbank: e.tensor_tensor(
                        msb[:, 0:4, :], esb[:, 0:4, :],
                        psA[:, gbank, 0:384].unsqueeze(1).broadcast_to([128, 4, 384]),
                        ALU.mult), "sV")
                    mdone = b.c("sV")

                    for j in range(2):
                        kk = 2 * hf + j
                        k = 4 * g + kk
                        ybank = kk % 2
                        b.wait("gpsimd", "sA", ring_free[k % 2])
                        b.wait("gpsimd", dtrt_done[k % 2][0], dtrt_done[k % 2][1])
                        b.op("gpsimd", lambda e, c=c, k=k: e.tensor_mul(
                            ring[:, k % 2, 0:CH], hbc[:, k, CH * c:CH * (c + 1)],
                            dtrt[:, k % 2, :]), "sP")
                        b.wait("tensor", "sP", b.c("sP"))
                        for lh in range(2):
                            b.op("tensor", lambda e, k=k, lh=lh: e.transpose(
                                psT[:, 2 * (k % 2) + lh, :],
                                ring[:, k % 2, 128 * lh:128 * (lh + 1)], idb[:, :]), "sT")
                        b.wait("scalar", "sT", b.c("sT"))
                        b.op("scalar", lambda e, k=k: e.copy(
                            ttile[:, 2 * (k % 2):2 * (k % 2) + 2, :].rearrange("p a b -> p (a b)"),
                            psT[:, 2 * (k % 2):2 * (k % 2) + 2, :].rearrange("p a b -> p (a b)")), "sA")
                        tev = b.c("sA")
                        ring_free[k % 2] = tev
                        b.wait("tensor", "sA", tev)
                        b.wait("tensor", "sV", mdone)
                        for hh2 in range(2):
                            h = 2 * k + hh2
                            b.op("tensor", lambda e, c=c, h=h, g=g, ybank=ybank, hh2=hh2, s_c=s_c: e.matmul(
                                psA[64 * hh2:64 * hh2 + 64, ybank, 0:256],
                                s_c[:, 64 * h:64 * (h + 1)],
                                hbc[:, 40 + g, CH * c:CH * (c + 1)],
                                start=True, stop=False), "sT" if hh2 == 1 else None)
                        b.wait("vector", "sT", b.c("sT"))
                        b.op("vector", lambda e, ybank=ybank, k=k: e.tensor_mul(
                            psA[:, ybank, 0:256], psA[:, ybank, 0:256],
                            ert[:, k % 2, :]), "sV")
                        ysc = b.c("sV")
                        b.wait("tensor", "sV", ysc)
                        sl = k % 2
                        for hh2 in range(2):
                            hg = (2 * kk + hh2) % 4
                            b.op("tensor", lambda e, hh2=hh2, hg=hg, sl=sl, ybank=ybank: e.matmul(
                                psA[64 * hh2:64 * hh2 + 64, ybank, 0:256],
                                ttile[:, 2 * sl, 64 * hh2:64 * hh2 + 64],
                                msb[:, hg, 0:256], start=False, stop=False), None)
                            b.op("tensor", lambda e, hh2=hh2, hg=hg, sl=sl, ybank=ybank: e.matmul(
                                psA[64 * hh2:64 * hh2 + 64, ybank, 128:256],
                                ttile[:, 2 * sl + 1, 64 * hh2:64 * hh2 + 64],
                                msb[:, hg, 256:384], start=False, stop=True),
                                "sT" if hh2 == 1 else None)
                        ydg = b.c("sT")
                        b.wait("vector", "sT", ydg)
                        b.op("vector", lambda e, c=c, k=k, ybank=ybank: e.scalar_tensor_tensor(
                            ysF[:, :], hbc[:, k, CH * c:CH * (c + 1)],
                            drep2[:, k:k + 1], psA[:, ybank, 0:256],
                            ALU.mult, ALU.add), "sV")
                        b.wait("vector", "sG", gcl)
                        b.op("vector", lambda e, c=c, k=k: e.tensor_mul(
                            hbc[:, k, CH * c:CH * (c + 1)], ysF[:, :],
                            gatec[:, k, :]), "sV")
                        yev = b.c("sV")
                        b.wait("tensor", "sV", yev)
                        b.wait("gpsimd", "sV", yev)
                        if k + 2 < 32:
                            load_dtrt(c, k + 2)
                    half_free[slot] = b.c("sV")
                    if hidx + 2 < len(halfq):
                        load_half(hidx + 2)
            prev_chunk_sV = b.c("sV")

        b.barrier()
        sqm = {}
        for k in range(32):
            if k >= 2:
                b.wait("vector", "sT", sqm[k - 2])
            b.op("vector", lambda e, k=k: e.tensor_mul(
                ring[:, k % 2, 0:T], hbc[:, k, :], hbc[:, k, :]), "sV")
            b.wait("tensor", "sV", b.c("sV"))
            for t in range(2):
                b.op("tensor", lambda e, k=k, t=t: e.matmul(
                    psA[0:1, 2 + t, 0:512], onescb[:, :], ring[:, k % 2, 512 * t:512 * (t + 1)],
                    start=(k == 0), stop=(k == 31)), "sT" if t == 1 else None)
            sqm[k] = b.c("sT")
        b.wait("vector", "sT", b.c("sT"))
        stats_rep(GRP2, INTER, EPS_RMS, False, bf_dst=reps)
        for k in range(32):
            b.op("vector", lambda e, k=k: e.tensor_scalar(
                hbc[:, k, :], hbc[:, k, :], nwS[:, k:k + 1], None, ALU.mult), "sV")
        ynw = b.c("sV")
        b.wait("tensor", "sV", ynw)

        bank_free6 = {j: 0 for j in range(4)}
        for i in range(16):
            b.wait("tensor", opw_done[i][0], opw_done[i][1])
            for t in range(2):
                bank = (2 * i + t) % 4
                b.wait("tensor", "sV", bank_free6[bank])
                for d in range(32):
                    b.op("tensor", lambda e, i=i, t=t, d=d, bank=bank: e.matmul(
                        psA[:, bank, :],
                        wb1[:, 4096 * (i % 2) + 128 * d:4096 * (i % 2) + 128 * (d + 1)],
                        hbc[:, d, 512 * t:512 * (t + 1)], start=(d == 0), stop=(d == 31)),
                        "sT" if d == 31 else None)
                b.wait("vector", "sT", b.c("sT"))
                b.op("vector", lambda e, i=i, t=t, bank=bank: e.tensor_mul(
                    hbc[:, 32 + i, 512 * t:512 * (t + 1)],
                    psA[:, bank, :], reps[:, 0, 512 * t:512 * (t + 1)]), "sV")
                bank_free6[bank] = b.c("sV")
            opw_cons.append(b.c("sT"))
            if i + 2 < 16:
                load_op_block(i + 2)

        sqm = {}
        for k in range(16):
            if k >= 2:
                b.wait("vector", "sT", sqm[k - 2])
            b.op("vector", lambda e, k=k: e.tensor_mul(
                ring[:, k % 2, 0:T], hbc[:, 32 + k, :], hbc[:, 32 + k, :]), "sV")
            b.wait("tensor", "sV", b.c("sV"))
            for t in range(2):
                b.op("tensor", lambda e, k=k, t=t: e.matmul(
                    psA[0:1, t, 0:512], onescb[:, :], hbc[:, 32 + k, 512 * t:512 * (t + 1)],
                    start=(k == 0), stop=(k == 15)), None)
                b.op("tensor", lambda e, k=k, t=t: e.matmul(
                    psA[0:1, 2 + t, 0:512], onescb[:, :], ring[:, k % 2, 512 * t:512 * (t + 1)],
                    start=(k == 0), stop=(k == 15)), "sT" if t == 1 else None)
            sqm[k] = b.c("sT")
        b.wait("vector", "sT", b.c("sT"))
        stats_rep(GRP2, HID, EPS_LN, True, f_dst=repf)

        for d in range(16):
            b.op("sync", lambda e, d=d: e.dma_start(
                out=t64[:, d % 2, :], in_=xT[128 * d:128 * (d + 1), 1:1 + T]), "sWs")
            xl = b.c("sWs")
            b.op("vector", lambda e, d=d: e.tensor_mul(
                cacc[:, d % 2, :], hbc[:, 32 + d, :], repf[:, 0, :]), "sV")
            b.op("vector", lambda e, d=d: e.tensor_sub(
                cacc[:, d % 2, :], cacc[:, d % 2, :], repf[:, 1, :]), "sV")
            b.wait("vector", "sWs", xl)
            b.op("vector", lambda e, d=d: e.tensor_add(
                t64[:, d % 2, :], t64[:, d % 2, :], cacc[:, d % 2, :]), "sV")
            b.op("vector", lambda e, d=d: e.tensor_copy(xb[:, d, 0:T], t64[:, d % 2, :]), "sV")
            x1v = b.c("sV")
            b.wait("sync", "sV", x1v)
            b.op("sync", lambda e, d=d: e.dma_start(out=x1D[d, :, :], in_=t64[:, d % 2, :]), "sWs")
            b.wait("vector", "sWs", b.c("sWs"))

        b.barrier()
        sqm = {}
        for d in range(16):
            if d >= 2:
                b.wait("vector", "sT", sqm[d - 2])
            b.op("vector", lambda e, d=d: e.tensor_mul(
                ring[:, d % 2, 0:T], xb[:, d, 0:T], xb[:, d, 0:T]), "sV")
            b.wait("tensor", "sV", b.c("sV"))
            for t in range(2):
                b.op("tensor", lambda e, d=d, t=t: e.matmul(
                    psA[0:1, t, 0:512], onescb[:, :], xb[:, d, 512 * t:512 * (t + 1)],
                    start=(d == 0), stop=(d == 15)), None)
                b.op("tensor", lambda e, d=d, t=t: e.matmul(
                    psA[0:1, 2 + t, 0:512], onescb[:, :], ring[:, d % 2, 512 * t:512 * (t + 1)],
                    start=(d == 0), stop=(d == 15)), "sT" if t == 1 else None)
            sqm[d] = b.c("sT")
        b.wait("vector", "sT", b.c("sT"))
        stats_rep(GRP2, HID, EPS_LN, True, bf_dst=reps)
        for d in range(16):
            b.op("vector", lambda e, d=d: e.tensor_mul(
                ring[:, d % 2, 0:T], xb[:, d, 0:T], reps[:, 0, 0:T]), "sV")
            b.op("vector", lambda e, d=d: e.tensor_sub(
                xb[:, d, 0:T], ring[:, d % 2, 0:T], reps[:, 1, 0:T]), "sV")
        xn2_done = b.c("sV")
        b.wait("tensor", "sV", xn2_done)
        for d in range(16):
            for t in range(2):
                b.op("tensor", lambda e, d=d, t=t: e.matmul(
                    psA[0:1, 5 + t, 0:512], onescb[:, :], xb[:, d, 512 * t:512 * (t + 1)],
                    start=(d == 0), stop=(d == 15)),
                    "sT" if (d == 15 and t == 1) else None)
        b.wait("vector", "sT", b.c("sT"))
        for t in range(2):
            b.op("vector", lambda e, t=t: e.tensor_scalar(
                corr[0:1, 0, 512 * t:512 * (t + 1)], psA[0:1, 5 + t, 0:512],
                -128.0, None, ALU.mult), "sV")
        corr1 = b.c("sV")

        wgR = wg.rearrange("(dc p) h -> p dc h", p=128)
        wuR = wu.rearrange("(dc p) h -> p dc h", p=128)
        fw_done = []
        fw_cons = []

        def load_fw_block(i):
            src = wgR if i % 2 == 0 else wuR
            ht = i // 2
            sem = "sW%d" % (i % 4)
            if i >= 4:
                b.wait("gpsimd", "sT", fw_cons[i - 4])
            b.op("gpsimd", lambda e, i=i, src=src, ht=ht: e.dma_start(
                out=wb1[:, 2048 * (i % 4):2048 * (i % 4) + 2048]
                .rearrange("p (dc h) -> p dc h", dc=16),
                in_=src[:, :, 128 * ht:128 * (ht + 1)]), sem)
            fw_done.append((sem, b.c(sem)))

        b.barrier(("gpsimd",))
        for i in range(4):
            load_fw_block(i)
        b.wait("scalar", "sV", corr1)
        b.wait("tensor", "sV", corr1)
        bank_free7 = {j: 0 for j in range(4)}
        pair_free = {0: 0, 1: 0}
        for ht in range(44):
            pr = ht % 2
            for gu_i in range(2):
                i = 2 * ht + gu_i
                b.wait("tensor", fw_done[i][0], fw_done[i][1])
                for t in range(2):
                    bank = (2 * i + t) % 4
                    b.wait("tensor", "sA", bank_free7[bank])
                    for d in range(16):
                        b.op("tensor", lambda e, i=i, d=d, t=t, bank=bank: e.matmul(
                            psA[:, bank, :],
                            wb1[:, 2048 * (i % 4) + 128 * d:2048 * (i % 4) + 128 * (d + 1)],
                            xb[:, d, 512 * t:512 * (t + 1)], start=(d == 0), stop=False), None)
                    b.op("tensor", lambda e, t=t, bank=bank: e.matmul(
                        psA[:, bank, :], ones1f[:, :], corr[0:1, 0, 512 * t:512 * (t + 1)],
                        start=False, stop=True), "sT")
                    b.wait("scalar", "sT", b.c("sT"))
                    if gu_i == 0 and t == 0:
                        b.wait("scalar", "sV", pair_free[pr])
                    if gu_i == 0:
                        b.op("scalar", lambda e, ht=ht, t=t, pr=pr, bank=bank: e.activation(
                            t64x4[:, 2 * pr, 512 * t:512 * (t + 1)], psA[:, bank, :], AF.Silu,
                            scale=sgS[:, ht:ht + 1]), "sA")
                    else:
                        b.op("scalar", lambda e, ht=ht, t=t, pr=pr, bank=bank: e.activation(
                            t64x4[:, 2 * pr + 1, 512 * t:512 * (t + 1)], psA[:, bank, :], AF.Copy,
                            scale=suS[:, ht:ht + 1]), "sA")
                    bank_free7[bank] = b.c("sA")
                fw_cons.append(b.c("sT"))
                if i + 4 < 88:
                    load_fw_block(i + 4)
            b.wait("vector", "sA", b.c("sA"))
            b.op("vector", lambda e, ht=ht, pr=pr: e.tensor_mul(
                hbc[:, ht, :], t64x4[:, 2 * pr, :], t64x4[:, 2 * pr + 1, :]), "sV")
            pair_free[pr] = b.c("sV")

        b.barrier()
        for d in range(44):
            for t in range(2):
                b.op("tensor", lambda e, d=d, t=t: e.matmul(
                    psA[0:1, 5 + t, 0:512], onescb[:, :], hbc[:, d, 512 * t:512 * (t + 1)],
                    start=(d == 0), stop=(d == 43)),
                    "sT" if (d == 43 and t == 1) else None)
        b.wait("vector", "sT", b.c("sT"))
        for t in range(2):
            b.op("vector", lambda e, t=t: e.tensor_scalar(
                corr[0:1, 1, 512 * t:512 * (t + 1)], psA[0:1, 5 + t, 0:512],
                -128.0, None, ALU.mult), "sV")
        corr2 = b.c("sV")

        wdR = wd.rearrange("(dc p) h -> p dc h", p=128)
        dw_done = []
        dw_cons = []
        wb1dw = wb1[:, 0:5632].rearrange("p (dc h) -> p dc h", dc=44)
        wb2dw = wb2.rearrange("p a b c -> p (a b c)")[:, 0:5632].rearrange(
            "p (dc h) -> p dc h", dc=44)

        def load_dw_block(i):
            sem = "sW%d" % (i % 2)
            dst = wb1dw if i % 2 == 0 else wb2dw
            if i >= 2:
                b.wait("gpsimd", "sT", dw_cons[i - 2])
            b.op("gpsimd", lambda e, i=i, dst=dst: e.dma_start(
                out=dst[:, :, :], in_=wdR[:, :, 128 * i:128 * (i + 1)]), sem)
            dw_done.append((sem, b.c(sem)))

        b.barrier(("gpsimd",))
        load_dw_block(0)
        load_dw_block(1)
        b.wait("scalar", "sV", corr2)
        b.wait("tensor", "sV", corr2)
        bank_free8 = {j: 0 for j in range(4)}
        for i in range(16):
            b.wait("tensor", dw_done[i][0], dw_done[i][1])
            wsl = wb1dw if i % 2 == 0 else wb2dw
            for t in range(2):
                bank = (2 * i + t) % 4
                b.wait("tensor", "sA", bank_free8[bank])
                for d in range(44):
                    b.op("tensor", lambda e, d=d, t=t, bank=bank, wsl=wsl: e.matmul(
                        psA[:, bank, :], wsl[:, d, :],
                        hbc[:, d, 512 * t:512 * (t + 1)], start=(d == 0), stop=False), None)
                b.op("tensor", lambda e, t=t, bank=bank: e.matmul(
                    psA[:, bank, :], ones1f[:, :], corr[0:1, 1, 512 * t:512 * (t + 1)],
                    start=False, stop=True), "sT")
                b.wait("scalar", "sT", b.c("sT"))
                b.op("scalar", lambda e, i=i, t=t, bank=bank: e.activation(
                    xb[:, i, 512 * t:512 * (t + 1)],
                    psA[:, bank, :], AF.Copy,
                    scale=sdS[:, i:i + 1]), "sA")
                bank_free8[bank] = b.c("sA")
            dw_cons.append(b.c("sT"))
            if i + 2 < 16:
                load_dw_block(i + 2)
        y2_done = b.c("sA")

        b.wait("vector", "sA", y2_done)
        b.wait("tensor", "sA", y2_done)
        sqm = {}
        for d in range(16):
            if d >= 2:
                b.wait("vector", "sT", sqm[d - 2])
            b.op("vector", lambda e, d=d: e.tensor_mul(
                ring[:, d % 2, 0:T], xb[:, d, 0:T], xb[:, d, 0:T]), "sV")
            b.wait("tensor", "sV", b.c("sV"))
            for t in range(2):
                b.op("tensor", lambda e, d=d, t=t: e.matmul(
                    psA[0:1, t, 0:512], onescb[:, :], xb[:, d, 512 * t:512 * (t + 1)],
                    start=(d == 0), stop=(d == 15)), None)
                b.op("tensor", lambda e, d=d, t=t: e.matmul(
                    psA[0:1, 2 + t, 0:512], onescb[:, :], ring[:, d % 2, 512 * t:512 * (t + 1)],
                    start=(d == 0), stop=(d == 15)), "sT" if t == 1 else None)
            sqm[d] = b.c("sT")
        b.wait("vector", "sT", b.c("sT"))
        stats_rep(GRP2, HID, EPS_LN, True, f_dst=repf)

        for d in range(16):
            b.op("sync", lambda e, d=d: e.dma_start(out=t64[:, d % 2, :], in_=x1D[d, :, :]), "sWs")
            xl = b.c("sWs")
            b.op("vector", lambda e, d=d: e.tensor_mul(
                cacc[:, d % 2, :], xb[:, d, 0:T], repf[:, 0, :]), "sV")
            b.op("vector", lambda e, d=d: e.tensor_sub(
                cacc[:, d % 2, :], cacc[:, d % 2, :], repf[:, 1, :]), "sV")
            b.wait("vector", "sWs", xl)
            b.op("vector", lambda e, d=d: e.tensor_add(
                t64[:, d % 2, :], t64[:, d % 2, :], cacc[:, d % 2, :]), "sV")
            ov = b.c("sV")
            b.wait("sync", "sV", ov)
            b.op("sync", lambda e, d=d: e.dma_start(
                out=outT[128 * d:128 * (d + 1), :], in_=t64[:, d % 2, :]), "sWs")
            b.wait("vector", "sWs", b.c("sWs"))

        b.barrier()

        @block.sync
        def _(e):
            for gfn in b.prog["sync"]:
                gfn(e, sems)

        @block.gpsimd
        def _(e):
            for gfn in b.prog["gpsimd"]:
                gfn(e, sems)

        @block.tensor
        def _(e):
            for gfn in b.prog["tensor"]:
                gfn(e, sems)

        @block.vector
        def _(e):
            for gfn in b.prog["vector"]:
                gfn(e, sems)

        @block.scalar
        def _(e):
            for gfn in b.prog["scalar"]:
                gfn(e, sems)

    return nc


_NC_CACHE = [None]


def kernel(**inp):
    f32 = np.float32
    x = np.asarray(inp["L0_attn_input_states"], dtype=f32)
    wipv = np.ascontiguousarray(np.asarray(inp["L0_attn_in_proj_w"], dtype=f32))
    wcvv = np.asarray(inp["L0_attn_conv_w"], dtype=f32)[:, 0, :]
    wcbv = np.asarray(inp["L0_attn_conv_b"], dtype=f32)
    dtbv = np.asarray(inp["L0_attn_dt_bias"], dtype=f32)
    alogv = np.asarray(inp["L0_attn_A_log"], dtype=f32)
    dv = np.asarray(inp["L0_attn_D"], dtype=f32)
    nwv = np.asarray(inp["L0_attn_norm_w"], dtype=f32)
    wopv = np.ascontiguousarray(np.asarray(inp["L0_attn_out_proj_w"], dtype=f32))
    wgq = np.ascontiguousarray(np.asarray(inp["L0_ffn_wq_gate"], dtype=np.int32).T)
    wuq = np.ascontiguousarray(np.asarray(inp["L0_ffn_wq_up"], dtype=np.int32).T)
    wdq = np.ascontiguousarray(np.asarray(inp["L0_ffn_wq_down"], dtype=np.int32).T)
    sgv = np.asarray(inp["L0_ffn_s_gate"], dtype=f32)
    suv = np.asarray(inp["L0_ffn_s_up"], dtype=f32)
    sdv = np.asarray(inp["L0_ffn_s_down"], dtype=f32)

    wcvS = np.ascontiguousarray(wcvv.reshape(4, 48, 128).transpose(2, 1, 0))
    wcbS = np.ascontiguousarray(wcbv.reshape(48, 128).T)
    nwS = np.ascontiguousarray(nwv.reshape(32, 128).T)
    sgS = np.ascontiguousarray(sgv.reshape(44, 128).T)
    suS = np.ascontiguousarray(suv.reshape(44, 128).T)
    sdS = np.ascontiguousarray(sdv.reshape(16, 128).T)
    dv2 = np.empty((128, 32), dtype=f32)
    dv2[0:64, :] = dv[0::2][None, :]
    dv2[64:128, :] = dv[1::2][None, :]

    if _NC_CACHE[0] is None:
        _NC_CACHE[0] = build_nc()
    nc = _NC_CACHE[0]
    in_maps = []
    for c in range(8):
        bb, half = c // 2, c % 2
        t0 = half * T
        xs = np.zeros((HID, TE), dtype=f32)
        lo, hi = t0 - 1, t0 + T + 2
        clo, chi = max(lo, 0), min(hi, 2048)
        xs[:, clo - lo:chi - lo] = x[bb].T[:, clo:chi]
        ws = np.zeros((128, 8), dtype=f32)
        if half == 1:
            ws[:, c - 1] = 1.0
        in_maps.append({
            "xT": xs, "wip": wipv, "wop": wopv, "wcv": wcvS, "wcb": wcbS,
            "dtb": dtbv.reshape(64, 1), "alog": alogv.reshape(64, 1),
            "dvec": dv.reshape(1, 64), "dvec2": dv2, "nw": nwS,
            "wg": wgq, "wu": wuq, "wd": wdq, "sg": sgS, "su": suS, "sdn": sdS,
            "wsel": ws,
        })
    res = run_bass_kernel_spmd(nc, in_maps, core_ids=list(range(8)))
    out = np.empty((4, 2048, HID), dtype=f32)
    for c in range(8):
        bb, half = c // 2, c % 2
        out[bb, half * T:(half + 1) * T, :] = res.results[c]["outT"].T
    return out
